# version 13
# baseline (speedup 1.0000x reference)
import numpy as np
from contextlib import ExitStack

import concourse.bass as bass
import concourse.mybir as mybir
from concourse.bass_utils import run_bass_kernel_spmd

F32 = mybir.dt.float32
BF16 = mybir.dt.bfloat16
AF = mybir.ActivationFunctionType
ALU = mybir.AluOpType
AX = mybir.AxisListType

B, T, D, H, HD = 4, 2048, 4096, 32, 128
NT = 16
ES = [16, 14, 12, 10, 8, 6, 4, 2]
RS = float(1.0 / np.sqrt(128.0))


class Sem:
    def __init__(self, nc, name):
        self.h = nc.alloc_semaphore(name)
        self.v = 0

    def inc(self, instr, n=1):
        if instr is not None:
            instr.then_inc(self.h, n)
        self.v += n
        return self.v


def pieces(width, maxw=512):
    out, off = [], 0
    while off < width:
        w = min(maxw, width - off)
        out.append((off, w))
        off += w
    return out


def slot_chunks(E):
    if E > 8:
        return [(0, 8 * 128), (8 * 128, (E - 8) * 128)]
    return [(0, E * 128)]


def build_nc():
    nc = bass.Bass("TRN2", target_bir_lowering=False)

    xtok = nc.dram_tensor("xtok", [T, D], F32, kind="ExternalInput").ap()
    xTd = nc.dram_tensor("xT", [D, 1024], F32, kind="ExternalInput").ap()
    wqT = nc.dram_tensor("wqT", [H, 128, D], F32, kind="ExternalInput").ap()
    wkT = nc.dram_tensor("wkT", [128, D], F32, kind="ExternalInput").ap()
    wvT = nc.dram_tensor("wvT", [128, D], F32, kind="ExternalInput").ap()
    woT = nc.dram_tensor("woT", [32, 128, D], F32, kind="ExternalInput").ap()
    wfTd = nc.dram_tensor("wfT", [32, 128, D], F32, kind="ExternalInput").ap()
    wfsl = nc.dram_tensor("wfsl", [128, 16384], F32, kind="ExternalInput").ap()
    wqb = nc.dram_tensor("wqb", [128, 32], F32, kind="ExternalInput").ap()
    wkb = nc.dram_tensor("wkb", [128, 1], F32, kind="ExternalInput").ap()
    wvb = nc.dram_tensor("wvb", [128, 1], F32, kind="ExternalInput").ap()
    wob = nc.dram_tensor("wob", [128, 32], F32, kind="ExternalInput").ap()
    fbb = nc.dram_tensor("fbb", [128, 32], F32, kind="ExternalInput").ap()
    mask2 = nc.dram_tensor("mask2", [128, 256], F32, kind="ExternalInput").ap()
    identb = nc.dram_tensor("identb", [128, 128], BF16, kind="ExternalInput").ap()
    identf = nc.dram_tensor("identf", [128, 128], F32, kind="ExternalInput").ap()
    outT = nc.dram_tensor("outT", [D, 1024], F32, kind="ExternalOutput").ap()

    cc_in = nc.dram_tensor("cc_in", [1, 2], F32, kind="Internal")
    cc_out = nc.dram_tensor("cc_out", [8, 2], F32, kind="Internal")

    es = ExitStack()
    es_kv = ExitStack()
    es0 = ExitStack()
    es2 = ExitStack()
    es2a = ExitStack()
    es2ps = ExitStack()
    es3 = ExitStack()

    def sbt(name, shape, dt, stack):
        return stack.enter_context(nc.sbuf_tensor(name, shape, dt))

    def pst(name, shape, dt, stack):
        return stack.enter_context(nc.psum_tensor(name, shape, dt))

    xnTo = sbt("xnTo", [128, 32 * 1024], BF16, es)
    m2 = sbt("m2", [128, 256], F32, es)
    idb = sbt("idb", [128, 128], BF16, es)
    idf = sbt("idf", [128, 128], F32, es)
    qb_s = sbt("qb_s", [128, 32], F32, es)
    qb2 = sbt("qb2", [128, 32], F32, es)
    kb_s = sbt("kb_s", [128, 1], F32, es)
    vb_s = sbt("vb_s", [128, 1], F32, es)
    ob_s = sbt("ob_s", [128, 32], F32, es)
    fb_s = sbt("fb_s", [128, 32], F32, es)
    obias = sbt("obias", [128, 32], F32, es)
    ones_c = sbt("ones_c", [128, 1], F32, es)
    ones8 = sbt("ones8", [8, 1], F32, es)
    ones_r = sbt("ones_r", [1, 128], F32, es)
    ones_cb = sbt("ones_cb", [128, 1], BF16, es)
    consts = sbt("consts", [128, 4], F32, es)
    absrows = sbt("absrows", [128, 16], F32, es)
    wfabs = sbt("wfabs", [128, 8], F32, es)
    cols2 = sbt("cols2", [128, 2], F32, es)
    tot_sb = sbt("tot_sb", [1, 2], F32, es)
    g_sb = sbt("g_sb", [8, 2], F32, es)
    row4 = sbt("row4", [1, 4], F32, es)

    kTb = sbt("kTb", [128, T], BF16, es_kv)
    vtok = sbt("vtok", [128, NT * 128], BF16, es_kv)

    xnTo_r = xnTo.ap().rearrange("p (dc t) -> p dc t", dc=32)

    s_ldx = [Sem(nc, "s_ldx0"), Sem(nc, "s_ldx1")]
    s_wfp = [Sem(nc, "s_wfp0"), Sem(nc, "s_wfp1")]
    s_wq = [Sem(nc, "s_wq0"), Sem(nc, "s_wq1")]
    s_wo = [Sem(nc, "s_wo0"), Sem(nc, "s_wo1")]
    s_wf3 = [Sem(nc, "s_wf30"), Sem(nc, "s_wf31")]
    s_xt3 = [Sem(nc, "s_xt30"), Sem(nc, "s_xt31")]
    s_st2 = [Sem(nc, "s_st20"), Sem(nc, "s_st21")]
    s_ldf = Sem(nc, "s_ldf")
    s_ldc = Sem(nc, "s_ldc")
    s_ldw = Sem(nc, "s_ldw")
    s_v = Sem(nc, "s_v")
    s_s = Sem(nc, "s_s")
    s_t = Sem(nc, "s_t")
    s_cc = Sem(nc, "s_cc")
    s_st = Sem(nc, "s_st")
    s_io = Sem(nc, "s_io")

    xt = [sbt(f"xt{i}", [128, D], F32, es0) for i in range(2)]
    xnt = [sbt(f"xnt{i}", [128, D], BF16, es0) for i in range(2)]
    xnf = [sbt(f"xnf{i}", [128, D], BF16, es0) for i in range(2)]
    wf0 = [sbt(f"wf0_{i}", [128, 2048], F32, es0) for i in range(2)]
    wk_sb = sbt("wk_sb", [128, D], BF16, es0)
    wv_sb = sbt("wv_sb", [128, D], BF16, es0)
    vtt = sbt("vtt", [128, 128], BF16, es0)
    bnst = sbt("bnst", [128, 48], F32, es0)
    stat = sbt("stat", [128, 8], F32, es0)
    dump = sbt("dump", [128, D], BF16, es0)
    kvK = pst("kvK", [128, 128], F32, es0)
    kvV = pst("kvV", [128, 128], F32, es0)
    tps = [pst(f"tps{i}", [128, 512], BF16, es0) for i in range(2)]
    vtp = pst("vtp", [128, 128], BF16, es0)
    tot_ps = pst("tot_ps", [1, 2], F32, es0)
    tot2_ps = pst("tot2_ps", [1, 2], F32, es0)
    cons_ps = pst("cons_ps", [128, 4], F32, es0)

    with nc.Block() as blk:
        @blk.sync
        def _(s):
            for dst, src in ((m2, mask2), (idb, identb), (idf, identf),
                             (qb_s, wqb), (kb_s, wkb), (vb_s, wvb),
                             (ob_s, wob), (fb_s, fbb)):
                s_ldc.inc(s.dma_start(out=dst[:], in_=src), 16)
            for tt in range(2):
                s_ldx[tt % 2].inc(s.dma_start(
                    out=xt[tt][:],
                    in_=xtok[tt * 128:(tt + 1) * 128, :]), 16)
            for ci in range(2):
                s_wfp[ci % 2].inc(s.dma_start(
                    out=wf0[ci][:],
                    in_=wfsl[:, ci * 2048:(ci + 1) * 2048]), 16)

        @blk.gpsimd
        def _(g):
            s_ldw.inc(g.dma_start(out=wk_sb[:], in_=wkT), 16)
            s_ldw.inc(g.dma_start(out=wv_sb[:], in_=wvT), 16)

        @blk.vector
        def _(v):
            v.memset(ones_c[:], 1.0)
            v.memset(ones_cb[:], 1.0)
            v.memset(ones8[:], 1.0)
            v.memset(ones_r[:], 1.0)

    ldc_all = s_ldc.v
    ldw_kv = s_ldw.v

    LP = {}

    def nwf(tt):
        return 1 if tt < 8 else 0

    def p0_pass(em):
        def sync_stream(s):
            for tt in range(NT):
                if tt + 2 < NT:
                    if em:
                        s.wait_ge(s_s.h, LP[("xn", tt)])
                    ins = None
                    if em:
                        ins = s.dma_start(
                            out=xt[tt % 2][:],
                            in_=xtok[(tt + 2) * 128:(tt + 3) * 128, :])
                    s_ldx[tt % 2].inc(ins, 16)
                ci = tt + 2
                if 2 <= ci < 8:
                    if em:
                        s.wait_ge(s_s.h, LP[("wfa", tt)])
                    ins = None
                    if em:
                        ins = s.dma_start(
                            out=wf0[ci % 2][:],
                            in_=wfsl[:, ci * 2048:(ci + 1) * 2048])
                    s_wfp[ci % 2].inc(ins, 16)

        def vector_stream(v):
            for it in range(NT + 2):
                if it < NT:
                    tt = it
                    p = tt % 2
                    if em:
                        v.wait_ge(s_ldx[p].h, 16 * (tt // 2 + 1))
                        if tt == 0:
                            v.wait_ge(s_ldc.h, ldc_all)
                        if tt >= 1:
                            v.wait_ge(s_v.h, LP[("aggr", tt - 1)])
                        if tt >= 1:
                            v.wait_ge(s_s.h, LP[("ln", tt - 1)])
                    last = None
                    for i in range(8):
                        if em:
                            last = v.bn_stats(bnst[:, i * 6:(i + 1) * 6],
                                              xt[p][:, i * 512:(i + 1) * 512])
                    lb = s_v.inc(last if em else None)
                    if em:
                        v.wait_ge(s_v.h, lb)
                    LP[("aggr", tt)] = s_v.inc(
                        v.bn_aggr(stat[:, 0:2], bnst[:]) if em else None)
                    if em:
                        v.wait_ge(s_v.h, LP[("aggr", tt)])
                    LP[("veps", tt)] = s_v.inc(
                        v.tensor_scalar(stat[:, 5:6], stat[:, 1:2], 1.0e-5,
                                        None, op0=ALU.add) if em else None)
                    if em:
                        v.wait_ge(s_s.h, LP[("rstd", tt)])
                        if tt >= 1:
                            v.wait_ge(s_s.h, LP[("xn", tt - 1)])
                    LP[("nmr", tt)] = s_v.inc(
                        v.scalar_tensor_tensor(
                            stat[:, 3:4], stat[:, 0:1], -1.0, stat[:, 2:3],
                            op0=ALU.mult, op1=ALU.mult) if em else None)
                if 1 <= it <= NT:
                    tt = it - 1
                    own = (tt % 2 == 0)
                    slot = tt // 2
                    p = tt % 2
                    for qd in range(8):
                        if em:
                            v.wait_ge(s_t.h, LP[("tp", tt, qd)])
                        dc0 = qd * 4
                        src_ = tps[qd % 2].ap().rearrange("p (a b) -> p a b", a=4)
                        ins = None
                        if own:
                            if em:
                                ins = v.tensor_copy(
                                    xnTo_r[:, dc0:dc0 + 4,
                                           slot * 128:(slot + 1) * 128], src_)
                        else:
                            if em:
                                if qd == 0 and tt >= 2:
                                    v.wait_ge(s_t.h, LP[("kvV", tt - 2)])
                                ins = v.tensor_copy(
                                    xnf[p].ap().rearrange(
                                        "p (a b) -> p a b",
                                        a=32)[:, dc0:dc0 + 4, :], src_)
                        LP[("ev", tt, qd)] = s_v.inc(ins)
                if 2 <= it:
                    tt = it - 2
                    if em:
                        v.wait_ge(s_t.h, LP[("vtp", tt)])
                    LP[("vte", tt)] = s_v.inc(
                        v.tensor_copy(vtok[:, tt * 128:(tt + 1) * 128], vtp[:])
                        if em else None)

        def scalar_stream(s_):
            for it in range(NT + 2):
                if it < NT:
                    tt = it
                    p = tt % 2
                    if em:
                        s_.wait_ge(s_v.h, LP[("veps", tt)])
                    LP[("ln", tt)] = s_s.inc(
                        s_.activation(stat[:, 4:5], stat[:, 5:6], AF.Ln,
                                      bias=0.0, scale=1.0) if em else None)
                    if em:
                        s_.wait_ge(s_s.h, LP[("ln", tt)])
                    LP[("rstd", tt)] = s_s.inc(
                        s_.activation(stat[:, 2:3], stat[:, 4:5], AF.Exp,
                                      bias=0.0, scale=-0.5) if em else None)
                    if em:
                        s_.wait_ge(s_v.h, LP[("nmr", tt)])
                        s_.wait_ge(s_s.h, LP[("rstd", tt)])
                        if tt >= 2:
                            s_.wait_ge(s_t.h, LP[("tp", tt - 2, 7)])
                    LP[("xn", tt)] = s_s.inc(
                        s_.activation(xnt[p][:], xt[p][:], AF.Identity,
                                      bias=stat[:, 3:4], scale=stat[:, 2:3])
                        if em else None)
                if 1 <= it <= NT:
                    tt = it - 1
                    p = tt % 2
                    if em:
                        s_.wait_ge(s_s.h, LP[("xn", tt)])
                        if tt >= 1:
                            s_.wait_ge(s_s.h, LP[("dmp", tt - 1)])
                    LP[("abs", tt)] = s_s.inc(
                        s_.activation(dump[:], xnt[p][:], AF.Abs,
                                      accum_out=absrows[:, tt:tt + 1])
                        if em else None)
                    LP[("dmp", tt)] = LP[("abs", tt)]
                    if nwf(tt):
                        if em:
                            s_.wait_ge(s_wfp[p].h, 16 * (tt // 2 + 1))
                            s_.wait_ge(s_s.h, LP[("abs", tt)])
                        LP[("wfa", tt)] = s_s.inc(
                            s_.activation(dump[:, 0:2048], wf0[p][:], AF.Abs,
                                          accum_out=wfabs[:, tt:tt + 1])
                            if em else None)
                        LP[("dmp", tt)] = LP[("wfa", tt)]
                    else:
                        LP[("wfa", tt)] = LP[("abs", tt)]
                if 2 <= it:
                    tt = it - 2
                    if em:
                        s_.wait_ge(s_t.h, LP[("kvK", tt)])
                    LP[("ke", tt)] = s_s.inc(
                        s_.activation(kTb[:, tt * 128:(tt + 1) * 128], kvK[:],
                                      AF.Identity, bias=kb_s[:]) if em else None)
                    if em:
                        s_.wait_ge(s_t.h, LP[("kvV", tt)])
                        if tt >= 1:
                            s_.wait_ge(s_t.h, LP[("vtp", tt - 1)])
                        s_.wait_ge(s_s.h, LP[("ke", tt)])
                    LP[("ve", tt)] = s_s.inc(
                        s_.activation(vtt[:], kvV[:], AF.Identity, bias=vb_s[:])
                        if em else None)

        def tensor_stream(t):
            for it in range(NT + 1):
                if it < NT:
                    tt = it
                    p = tt % 2
                    if em:
                        t.wait_ge(s_s.h, LP[("xn", tt)])
                        if tt == 0:
                            t.wait_ge(s_ldc.h, ldc_all)
                    for qd in range(8):
                        if em:
                            gq = 8 * tt + qd
                            if gq >= 2:
                                pt_, pq = (gq - 2) // 8, (gq - 2) % 8
                                t.wait_ge(s_v.h, LP[("ev", pt_, pq)])
                        last = None
                        for i in range(4):
                            dc = qd * 4 + i
                            if em:
                                last = t.transpose(
                                    tps[qd % 2][:, i * 128:(i + 1) * 128],
                                    xnt[p][:, dc * 128:(dc + 1) * 128], idb[:])
                        LP[("tp", tt, qd)] = s_t.inc(last if em else None)
                if it >= 1:
                    tt = it - 1
                    own = (tt % 2 == 0)
                    slot = tt // 2
                    p = tt % 2
                    if em:
                        t.wait_ge(s_v.h, LP[("ev", tt, 7)])
                        if tt == 0:
                            t.wait_ge(s_ldw.h, ldw_kv)
                        if tt >= 1:
                            t.wait_ge(s_s.h, LP[("ve", tt - 1)])
                    for wi, ps in ((0, kvK), (1, kvV)):
                        w = wk_sb if wi == 0 else wv_sb
                        last = None
                        for dc in range(32):
                            if em:
                                if own:
                                    rhs = xnTo.ap()[:, dc * 1024 + slot * 128:
                                                    dc * 1024 + (slot + 1) * 128]
                                else:
                                    rhs = xnf[p].ap()[:, dc * 128:(dc + 1) * 128]
                                last = t.matmul(ps[:],
                                                w[:, dc * 128:(dc + 1) * 128],
                                                rhs, start=(dc == 0),
                                                stop=(dc == 31))
                        LP[("kvK" if wi == 0 else "kvV", tt)] = s_t.inc(
                            last if em else None)
                    if em:
                        t.wait_ge(s_s.h, LP[("ve", tt)])
                        if tt >= 1:
                            t.wait_ge(s_v.h, LP[("vte", tt - 1)])
                    LP[("vtp", tt)] = s_t.inc(
                        t.transpose(vtp[:], vtt[:], idb[:]) if em else None)

        return sync_stream, vector_stream, scalar_stream, tensor_stream

    def _snap0():
        return (s_v.v, s_s.v, s_t.v, s_ldx[0].v, s_ldx[1].v,
                s_wfp[0].v, s_wfp[1].v)

    def _restore0(t_):
        (s_v.v, s_s.v, s_t.v, s_ldx[0].v, s_ldx[1].v,
         s_wfp[0].v, s_wfp[1].v) = t_

    save0 = _snap0()
    sy0, vv0, ss0, tt0 = p0_pass(False)
    sy0(None); vv0(None); ss0(None); tt0(None)
    endv0 = _snap0()
    _restore0(save0)
    sy0, vv0, ss0, tt0 = p0_pass(True)
    with nc.Block() as blk:
        blk.sync(sy0)
        blk.vector(vv0)
        blk.scalar(ss0)
        blk.tensor(tt0)
    assert _snap0() == endv0

    with nc.Block() as blk:
        @blk.vector
        def _(v):
            s_v.inc(v.reduce_sum(cols2[:, 0:1], absrows[:], axis=AX.X))
            s_v.inc(v.reduce_sum(cols2[:, 1:2], wfabs[:], axis=AX.X))
        v_cols = s_v.v

        @blk.tensor
        def _(t):
            t.wait_ge(s_v.h, v_cols)
            s_t.inc(t.matmul(tot_ps[:], ones_c[:], cols2[:], start=True, stop=True))
        t_tot = s_t.v

        @blk.vector
        def _(v):
            v.wait_ge(s_t.h, t_tot)
            s_v.inc(v.tensor_copy(tot_sb[:], tot_ps[:]))
        v_tot = s_v.v

        @blk.gpsimd
        def _(g):
            g.wait_ge(s_v.h, v_tot)
            s_io.inc(g.dma_start(out=cc_in.ap(), in_=tot_sb[:]), 16)
            g.wait_ge(s_io.h, s_io.v)
            s_cc.inc(g.collective_compute(
                "AllGather", ALU.bypass, replica_groups=[list(range(8))],
                ins=[cc_in.ap().opt()], outs=[cc_out.ap().opt()]))
            g.wait_ge(s_cc.h, s_cc.v)
            s_io.inc(g.dma_start(out=g_sb[:], in_=cc_out.ap()), 16)
        io_g = s_io.v

        @blk.tensor
        def _(t):
            t.wait_ge(s_io.h, io_g)
            s_t.inc(t.matmul(tot2_ps[:], ones8[:], g_sb[:], start=True, stop=True))
        t_tot2 = s_t.v

        @blk.vector
        def _(v):
            v.wait_ge(s_t.h, t_tot2)
            s_v.inc(v.tensor_scalar(row4[0:1, 0:1], tot2_ps[0:1, 0:1],
                                    1.0 / (2.0 * B * T * D), 1.0e-8,
                                    op0=ALU.mult, op1=ALU.max))
            b0 = s_v.inc(v.tensor_scalar(row4[0:1, 1:2], tot2_ps[0:1, 1:2],
                                         1.0 / (float(D) * D), 1.0e-8,
                                         op0=ALU.mult, op1=ALU.max))
            v.wait_ge(s_v.h, b0)
            b1 = s_v.inc(v.tensor_scalar(row4[0:1, 2:3], row4[0:1, 1:2], 0.5,
                                         None, op0=ALU.mult))
            v.wait_ge(s_v.h, b1)
            s_v.inc(v.tensor_scalar(row4[0:1, 3:4], row4[0:1, 2:3], -1.0,
                                    None, op0=ALU.mult))
        v_row4 = s_v.v

        @blk.tensor
        def _(t):
            t.wait_ge(s_v.h, v_row4)
            s_t.inc(t.matmul(cons_ps[:], ones_r[:], row4[:], start=True, stop=True))
        t_cons = s_t.v

        @blk.vector
        def _(v):
            v.wait_ge(s_t.h, t_cons)
            b2 = s_v.inc(v.tensor_copy(consts[:], cons_ps[:]))
            v.wait_ge(s_v.h, b2)
            s_v.inc(v.scalar_tensor_tensor(obias[:], fb_s[:], consts[:, 0:1],
                                           ob_s[:], op0=ALU.mult, op1=ALU.add))
            s_v.inc(v.tensor_scalar(qb2[:], qb_s[:], RS, None, op0=ALU.mult))

    es0.close()

    attnT = sbt("attnT", [128, 32 * 1024], BF16, es2)
    qTt = [sbt(f"qTt{i}", [128, 4 * 1024], BF16, es2a) for i in range(2)]
    wq_sb = [sbt(f"wq_sb{i}", [128, D], BF16, es2a) for i in range(2)]
    Pt = [sbt("Pt0", [128, 4 * 2048], BF16, es2a),
          sbt("Pt1", [128, 4 * 1792], BF16, es2a)]
    PS = [2048, 1792]
    sqb = Pt[1].ap()[:, 0:512]
    da_b = sbt("da_b", [128, 8], F32, es2a)
    db_b = sbt("db_b", [128, 8], F32, es2a)
    dsum = sbt("dsum", [128, 4], F32, es2a)
    rd8 = sbt("rd8", [128, 8], F32, es2a)
    qmx = sbt("qmx", [1, 8], F32, es2a)
    qm4 = sbt("qm4", [1, 4], F32, es2a)
    kmx = sbt("kmx", [1, 8], F32, es2a)
    c4r = sbt("c4r", [1, 4], F32, es2a)
    negc = sbt("negc", [128, 4], F32, es2a)
    msk_sb = [sbt(f"msk_sb{i}", [128, 256], F32, es2a) for i in range(2)]
    ptm_sb = [sbt(f"ptm_sb{i}", [128, 1024], BF16, es2a) for i in range(2)]


    sAB = [pst(f"sps{i}", [128, 1024], F32, es2ps) for i in range(2)]
    ptAB = [pst(f"ptps{i}", [128, 1024], BF16, es2ps) for i in range(2)]
    avq = pst("avq", [128, 512], F32, es2ps)
    qps = pst("qps", [128, 512], F32, es2ps)
    attnT_r = attnT.ap().rearrange("p (h t) -> p h t", h=32)

    gstate = {
        "sbank_free": [(s_s, 0), (s_s, 0)],
        "ptbank_free": [0, 0],
        "msk_free": [0, 0],
        "av_free": 0,
        "qps_free": (s_s, 0),
    }
    LBL = {}

    def q_unit(t, em, g, u):
        h, tc = u // 2, u % 2
        hg = 4 * g + h
        if em:
            t.wait_ge(s_wq[hg % 2].h, 16 * (hg // 2 + 1))
            qsem, qval = gstate["qps_free"]
            t.wait_ge(qsem.h, qval)
        last = None
        for dc in range(32):
            if em:
                last = t.matmul(
                    qps[:], wq_sb[hg % 2][:, dc * 128:(dc + 1) * 128],
                    xnTo.ap()[:, dc * 1024 + tc * 512:dc * 1024 + (tc + 1) * 512],
                    start=(dc == 0), stop=(dc == 31))
        LBL[("qmm", g, h, tc)] = s_t.inc(last if em else None)
        if em:
            gstate["qps_free"] = (s_s, LBL[("qev", g, h, tc)])

    def qev_unit(s_e, em, g, u):
        h, tc = u // 2, u % 2
        hg = 4 * g + h
        if em:
            s_e.wait_ge(s_t.h, LBL[("qmm", g, h, tc)])
        ins = None
        if em:
            ins = s_e.activation(
                qTt[g % 2][:, h * 1024 + tc * 512:h * 1024 + (tc + 1) * 512],
                qps[:], AF.Identity, bias=qb2[:, hg:hg + 1], scale=RS)
        LBL[("qev", g, h, tc)] = s_s.inc(ins)

    def group_pass(b, em):
        G = b

        def norms_prologue_T(t):
            if b == 0 and em:
                t.wait_ge(s_s.h, LBL[("qev", 0, 3, 1)])
            if b == 0:
                for j in range(4):
                    if em:
                        t.wait_ge(s_v.h, LBL[("ksq", j)])
                        if j > 0:
                            t.wait_ge(s_v.h, LBL[("krm", j - 1)])
                    ins = None
                    if em:
                        ins = t.matmul(qps[0:1, 0:512], ones_cb[:], sqb,
                                       start=True, stop=True)
                    LBL[("knm", j)] = s_t.inc(ins)
            for p in range(8):
                if em:
                    t.wait_ge(s_v.h, LBL[("qsq", G, p)])
                    if p > 0:
                        t.wait_ge(s_v.h, LBL[("qrm", G, p - 1)])
                    elif b == 0:
                        t.wait_ge(s_v.h, LBL[("krm", 3)])
                ins = None
                if em:
                    ins = t.matmul(qps[0:1, 0:512], ones_cb[:], sqb,
                                   start=True, stop=True)
                LBL[("qnm", G, p)] = s_t.inc(ins)
            if em:
                t.wait_ge(s_v.h, LBL[("ncrow", G)])
            ins = None
            if em:
                ins = t.matmul(avq[:, 0:4], ones_r[:], c4r[0:1, :],
                               start=True, stop=True)
            LBL[("ncrep", G)] = s_t.inc(ins)

        def tensor_stream(t):
            if b == 0:
                for u in range(8):
                    q_unit(t, em, 0, u)
            norms_prologue_T(t)
            for s in range(9):
                if s == 8:
                    emit_ptav(t, 7)
                    break
                if b < 7:
                    if s == 0 and em:
                        t.wait_ge(s_v.h, LBL[("qrm", G, 7)])
                    q_unit(t, em, b + 1, s)
                E = ES[s]
                ch = slot_chunks(E)
                nch = len(ch)
                steps = [(h, ci) for h in range(4) for ci in range(nch)]
                for idx, (h, ci) in enumerate(steps):
                    bb = LBL[("bank", G, s, idx)]
                    if em:
                        sem_, val_ = gstate["sbank_free"][bb]
                        t.wait_ge(sem_.h, val_)
                        t.wait_ge(s_s.h, LBL[("qev", G, h, 1)])
                    off0, cw = ch[ci]
                    last = None
                    for (off, w) in pieces(cw):
                        if em:
                            last = t.matmul(
                                sAB[bb][:, off:off + w],
                                qTt[G % 2][:, h * 1024 + s * 128:
                                           h * 1024 + (s + 1) * 128],
                                kTb[:, (16 - E) * 128 + off0 + off:
                                    (16 - E) * 128 + off0 + off + w],
                                start=True, stop=True)
                    LBL[("sc", G, s, idx)] = s_t.inc(last if em else None)
                    if em:
                        gstate["sbank_free"][bb] = (s_s, LBL[("exp", G, s, idx)])
                if s >= 1:
                    emit_ptav(t, s - 1)

        def emit_ptav(t, s):
            E = ES[s]
            nr = E // 2
            for r in range(nr):
                pb = LBL[("ptb", G, s, r)]
                ps_ = PS[s % 2]
                csem = s_v if pb == 0 else s_s
                if em:
                    t.wait_ge(csem.h, gstate["ptbank_free"][pb])
                    ci_hi = 0 if (2 * r + 1) < 8 else 1
                    t.wait_ge(s_v.h, LBL[("pn", G, s, ci_hi)])
                last = None
                for j in range(2):
                    kk = 2 * r + j
                    for h in range(4):
                        if em:
                            last = t.transpose(
                                ptAB[pb][:, j * 512 + h * 128:
                                         j * 512 + (h + 1) * 128],
                                Pt[s % 2][:, h * ps_ + kk * 128:
                                          h * ps_ + (kk + 1) * 128],
                                idb[:])
                LBL[("pt", G, s, r)] = s_t.inc(last if em else None)
                if em:
                    gstate["ptbank_free"][pb] = LBL[("ptc", G, s, r)]
                if r >= 1:
                    emit_av_round(t, s, r - 1)
            emit_av_round(t, s, nr - 1)
            if em:
                gstate["av_free"] = LBL[("avev", G, s)]

        def emit_av_round(t, s, r):
            E = ES[s]
            nr = E // 2
            pb = LBL[("ptb", G, s, r)]
            csem = s_v if pb == 0 else s_s
            if em:
                t.wait_ge(csem.h, LBL[("ptc", G, s, r)])
                if r == 0:
                    t.wait_ge(s_v.h, gstate["av_free"])
            for j in range(2):
                kk = 2 * r + j
                kabs = 16 - E + kk
                ins = None
                if em:
                    ins = t.matmul(avq[:],
                                   vtok[:, kabs * 128:(kabs + 1) * 128],
                                   ptm_sb[pb][:, j * 512:(j + 1) * 512],
                                   start=(kk == 0), stop=(kk == E - 1))
                LBL[("av", G, s)] = s_t.inc(ins)

        def emit_ptc(eng, s, want_pb):
            E = ES[s]
            nr = E // 2
            for r in range(nr):
                pb = LBL[("ptb", G, s, r)]
                if pb != want_pb:
                    continue
                if em:
                    eng.wait_ge(s_t.h, LBL[("pt", G, s, r)])
                ins = None
                if em:
                    if want_pb == 1:
                        ins = eng.copy(ptm_sb[pb][:], ptAB[pb][:])
                    else:
                        ins = eng.tensor_copy(ptm_sb[pb][:], ptAB[pb][:])
                sem = s_s if want_pb == 1 else s_v
                LBL[("ptc", G, s, r)] = sem.inc(ins)

        def scalar_stream(s_e):
            if b == 0:
                for u in range(8):
                    qev_unit(s_e, em, 0, u)
            if em:
                s_e.wait_ge(s_v.h, LBL[("qm4", G)])
            ins = s_e.activation(c4r[0:1, :], qm4[0:1, :], AF.Ln) if em else None
            l_ln = s_s.inc(ins)
            if em:
                s_e.wait_ge(s_s.h, l_ln)
            ins = None
            if em:
                ins = s_e.activation(c4r[0:1, :], c4r[0:1, :], AF.Exp, scale=0.5)
            LBL[("c4", G)] = s_s.inc(ins)
            for s in range(8):
                if b < 7:
                    qev_unit(s_e, em, b + 1, s)
                E = ES[s]
                ch = slot_chunks(E)
                nch = len(ch)
                steps = [(h, ci) for h in range(4) for ci in range(nch)]
                for idx, (h, ci) in enumerate(steps):
                    bu = LBL[("bank", G, s, idx)]
                    ps_ = PS[s % 2]
                    if em:
                        if ci == 0:
                            s_e.wait_ge(s_v.h, LBL[("msk", G, s, idx)])
                        else:
                            s_e.wait_ge(s_t.h, LBL[("sc", G, s, idx)])
                        if s == 0 and idx < 2:
                            s_e.wait_ge(s_v.h, LBL[("ncev", G)])
                    off0, cw = ch[ci]
                    col = ci * 4 + h
                    last = None
                    if ci == 0:
                        if em:
                            last = s_e.activation(
                                Pt[s % 2][:, h * ps_:h * ps_ + 256],
                                msk_sb[h % 2][:], AF.Exp,
                                bias=negc[:, h:h + 1], scale=1.0,
                                accum_out=da_b[:, col:col + 1])
                        if cw > 256:
                            if em:
                                last = s_e.activation(
                                    Pt[s % 2][:, h * ps_ + 256:h * ps_ + cw],
                                    sAB[bu][:, 256:cw], AF.Exp,
                                    bias=negc[:, h:h + 1], scale=1.0,
                                    accum_out=db_b[:, col:col + 1])
                    else:
                        if em:
                            last = s_e.activation(
                                Pt[s % 2][:, h * ps_ + 1024:h * ps_ + 1024 + cw],
                                sAB[bu][:, 0:cw], AF.Exp,
                                bias=negc[:, h:h + 1], scale=1.0,
                                accum_out=da_b[:, col:col + 1])
                    LBL[("exp", G, s, idx)] = s_s.inc(last if em else None)
                if s >= 1:
                    emit_ptc(s_e, s - 1, 1)
            emit_ptc(s_e, 7, 1)

        def emit_avev(v, s):
            if em:
                v.wait_ge(s_t.h, LBL[("av", G, s)])
            ins = None
            if em:
                ins = v.tensor_copy(
                    attnT_r[:, 4 * G:4 * G + 4, s * 128:(s + 1) * 128],
                    avq.ap().rearrange("p (a b) -> p a b", a=4))
            LBL[("avev", G, s)] = s_v.inc(ins)

        def vector_stream(v):
            if b == 0:
                for j in range(4):
                    if em:
                        if j > 0:
                            v.wait_ge(s_t.h, LBL[("knm", j - 1)])
                    ins = None
                    if em:
                        ins = v.tensor_mul(sqb, kTb[:, j * 512:(j + 1) * 512],
                                           kTb[:, j * 512:(j + 1) * 512])
                    LBL[("ksq", j)] = s_v.inc(ins)
                    if em:
                        v.wait_ge(s_t.h, LBL[("knm", j)])
                    ins = None
                    if em:
                        ins = v.reduce_max(kmx[0:1, j:j + 1], qps[0:1, 0:512],
                                           axis=AX.X)
                    LBL[("krm", j)] = s_v.inc(ins)
                if em:
                    v.wait_ge(s_v.h, LBL[("krm", 3)])
                ins = None
                if em:
                    ins = v.reduce_max(kmx[0:1, 4:5], kmx[0:1, 0:4], axis=AX.X)
                LBL[("kmax", 0)] = s_v.inc(ins)
            for p in range(8):
                h, j = p % 4, p // 4
                if em:
                    if b == 0:
                        v.wait_ge(s_s.h, LBL[("qev", G, h, 1)])
                    if p == 0 and b == 0:
                        v.wait_ge(s_t.h, LBL[("knm", 3)])
                    if p > 0:
                        v.wait_ge(s_t.h, LBL[("qnm", G, p - 1)])
                ins = None
                if em:
                    ins = v.tensor_mul(
                        sqb,
                        qTt[G % 2][:, h * 1024 + j * 512:h * 1024 + (j + 1) * 512],
                        qTt[G % 2][:, h * 1024 + j * 512:h * 1024 + (j + 1) * 512])
                LBL[("qsq", G, p)] = s_v.inc(ins)
                if em:
                    v.wait_ge(s_t.h, LBL[("qnm", G, p)])
                ins = None
                if em:
                    ins = v.reduce_max(qmx[0:1, p:p + 1], qps[0:1, 0:512],
                                       axis=AX.X)
                LBL[("qrm", G, p)] = s_v.inc(ins)
            if em:
                v.wait_ge(s_v.h, LBL[("qrm", G, 7)])
            ins = None
            if em:
                ins = v.tensor_tensor(qm4[0:1, :], qmx[0:1, 0:4],
                                      qmx[0:1, 4:8], op=ALU.max)
            l_qm = s_v.inc(ins)
            if em:
                v.wait_ge(s_v.h, l_qm)
                v.wait_ge(s_v.h, LBL[("kmax", 0)])
            ins = None
            if em:
                ins = v.tensor_scalar(qm4[0:1, :], qm4[0:1, :], kmx[0:1, 4:5],
                                      None, op0=ALU.mult)
            LBL[("qm4", G)] = s_v.inc(ins)
            if em:
                v.wait_ge(s_s.h, LBL[("c4", G)])
            ins = None
            if em:
                ins = v.tensor_scalar(c4r[0:1, :], c4r[0:1, :],
                                      -1.05 * RS, None, op0=ALU.mult)
            LBL[("ncrow", G)] = s_v.inc(ins)
            if em:
                v.wait_ge(s_t.h, LBL[("ncrep", G)])
            ins = v.tensor_copy(negc[:], avq[:, 0:4]) if em else None
            LBL[("ncev", G)] = s_v.inc(ins)
            if em and gstate["av_free"] < LBL[("ncev", G)]:
                gstate["av_free"] = LBL[("ncev", G)]
            for s in range(8):
                E = ES[s]
                ch = slot_chunks(E)
                nch = len(ch)
                steps = [(h, ci) for h in range(4) for ci in range(nch)]
                second0 = ch[0][1] > 256
                if second0:
                    s_v.inc(v.memset(db_b[:], 0.0) if em else None)
                for idx, (h, ci) in enumerate(steps):
                    bb = LBL[("bank", G, s, idx)]
                    if ci != 0:
                        continue
                    if em:
                        v.wait_ge(s_t.h, LBL[("sc", G, s, idx)])
                        v.wait_ge(s_s.h, gstate["msk_free"][h % 2])
                    ins = None
                    if em:
                        ins = v.tensor_add(msk_sb[h % 2][:],
                                           sAB[bb][:, 0:256], m2[:])
                    LBL[("msk", G, s, idx)] = s_v.inc(ins)
                    if em:
                        gstate["msk_free"][h % 2] = LBL[("exp", G, s, idx)]
                if s >= 1:
                    emit_ptc(v, s - 1, 0)
                    emit_avev(v, s - 1)
                if em:
                    v.wait_ge(s_s.h, LBL[("exp", G, s, len(steps) - 1)])
                if nch == 2:
                    l1 = s_v.inc(v.tensor_add(dsum[:], da_b[:, 0:4], db_b[:, 0:4])
                                 if em else None)
                    if em:
                        v.wait_ge(s_v.h, l1)
                    l2 = s_v.inc(v.tensor_add(dsum[:], dsum[:], da_b[:, 4:8])
                                 if em else None)
                    if em:
                        v.wait_ge(s_v.h, l2)
                    LBL[("rd8", G, s)] = s_v.inc(
                        v.reciprocal(rd8[:, 0:4], dsum[:]) if em else None)
                elif second0:
                    l1 = s_v.inc(v.tensor_add(dsum[:], da_b[:, 0:4],
                                              db_b[:, 0:4]) if em else None)
                    if em:
                        v.wait_ge(s_v.h, l1)
                    LBL[("rd8", G, s)] = s_v.inc(
                        v.reciprocal(rd8[:, 0:4], dsum[:]) if em else None)
                else:
                    LBL[("rd8", G, s)] = s_v.inc(
                        v.reciprocal(rd8[:, 0:4], da_b[:, 0:4]) if em else None)
                if em:
                    v.wait_ge(s_v.h, LBL[("rd8", G, s)])
                for ci in range(nch):
                    cw = ch[ci][1]
                    ps_ = PS[s % 2]
                    last = None
                    for h in range(4):
                        if em:
                            last = v.tensor_scalar(
                                Pt[s % 2][:, h * ps_ + ci * 1024:
                                          h * ps_ + ci * 1024 + cw],
                                Pt[s % 2][:, h * ps_ + ci * 1024:
                                          h * ps_ + ci * 1024 + cw],
                                rd8[:, h:h + 1], None,
                                op0=ALU.mult)
                    LBL[("pn", G, s, ci)] = s_v.inc(last if em else None)
            emit_ptc(v, 7, 0)
            emit_avev(v, 7)

        def gpsimd_stream(gp):
            if b == 0:
                heads = list(range(0, 10))
            else:
                heads = [hg for hg in range(4 * b + 6, 4 * b + 10) if hg < 32]
            for hg in heads:
                if hg >= 2:
                    pg, ph = (hg - 2) // 4, (hg - 2) % 4
                    if em:
                        gp.wait_ge(s_t.h, LBL[("qmm", pg, ph, 1)])
                ins = None
                if em:
                    ins = gp.dma_start(out=wq_sb[hg % 2][:], in_=wqT[hg])
                s_wq[hg % 2].inc(ins, 16)
        def sync_stream(s):
            pass

        return (tensor_stream, scalar_stream, vector_stream, gpsimd_stream,
                sync_stream)

    for b in range(8):
        if b == 0:
            sb_ctr = 0
            pt_ctr = 0
        else:
            sb_ctr = LBL[("sbctr", b - 1)]
            pt_ctr = LBL[("ptctr", b - 1)]
        for s in range(8):
            E = ES[s]
            nch = 2 if E > 8 else 1
            for idx in range(4 * nch):
                LBL[("bank", b, s, idx)] = sb_ctr % 2
                sb_ctr += 1
            for r in range(E // 2):
                LBL[("ptb", b, s, r)] = pt_ctr % 2
                pt_ctr += 1
        LBL[("sbctr", b)] = sb_ctr
        LBL[("ptctr", b)] = pt_ctr

        def _snap():
            return (s_v.v, s_s.v, s_t.v, s_wq[0].v, s_wq[1].v,
                    s_wo[0].v, s_wo[1].v, s_wf3[0].v, s_wf3[1].v,
                    s_xt3[0].v, s_xt3[1].v)

        def _restore(t_):
            (s_v.v, s_s.v, s_t.v, s_wq[0].v, s_wq[1].v,
             s_wo[0].v, s_wo[1].v, s_wf3[0].v, s_wf3[1].v,
             s_xt3[0].v, s_xt3[1].v) = t_

        save = _snap()
        ts, ss, vs, gs, sy = group_pass(b, False)
        ts(None); ss(None); vs(None); gs(None); sy(None)
        endv = _snap()
        _restore(save)
        ts, ss, vs, gs, sy = group_pass(b, True)
        with nc.Block() as blk:
            blk.tensor(ts)
            blk.scalar(ss)
            blk.vector(vs)
            blk.gpsimd(gs)
            blk.sync(sy)
        assert _snap() == endv, (b,)

    es2a.close()
    es2ps.close()

    wo_sb = [sbt(f"wo_sb{i}", [128, D], BF16, es3) for i in range(2)]
    wf3 = [sbt(f"wf3_{i}", [128, 2048], F32, es3) for i in range(2)]
    qa_sb = sbt("qa_sb", [128, 2048], BF16, es3)
    qb_sb = sbt("qb_sb", [128, 2048], BF16, es3)
    wq3 = [sbt(f"wq3_{i}", [128, D], BF16, es3) for i in range(2)]
    xT_sb = [sbt(f"xT_sb{i}", [128, 512], F32, es3) for i in range(2)]
    ffr_sb = sbt("ffr_sb", [128, 512], F32, es3)
    out_sb = [sbt(f"out_sb{i}", [128, 512], F32, es3) for i in range(2)]

    fps = [pst(f"fps{i}", [128, 512], F32, es3) for i in range(2)]
    ops_ = [pst(f"ops{i}", [128, 512], F32, es3) for i in range(2)]

    def b3_pass(em, L):
        def sync_stream(s):
            for tc in range(2):
                ins = None
                if em:
                    ins = s.dma_start(
                        out=xT_sb[tc][:],
                        in_=xTd[0:128, tc * 512:(tc + 1) * 512])
                L[("xtd", 0, tc)] = s_xt3[tc].inc(ins, 16)
            for cc in range(32):
                nx = cc + 1
                if nx < 32:
                    for tc in range(2):
                        if em:
                            s.wait_ge(s_v.h, L[("ffr", cc, tc)])
                        ins = None
                        if em:
                            ins = s.dma_start(
                                out=xT_sb[tc][:],
                                in_=xTd[nx * 128:(nx + 1) * 128,
                                        tc * 512:(tc + 1) * 512])
                        L[("xtd", nx, tc)] = s_xt3[tc].inc(ins, 16)
                for tc in range(2):
                    if em:
                        s.wait_ge(s_v.h, L[("fin", cc, tc)])
                    ins = None
                    if em:
                        ins = s.dma_start(
                            out=outT[cc * 128:(cc + 1) * 128,
                                     tc * 512:(tc + 1) * 512],
                            in_=out_sb[tc][:])
                    L[("st", cc, tc)] = s_st2[tc].inc(ins, 16)
            if em:
                s.wait_ge(s_st2[0].h, L[("st", 31, 0)])
                s.wait_ge(s_st2[1].h, L[("st", 31, 1)])

        def gpsimd_stream(gp):
            for half in range(2):
                ins = None
                if em:
                    ins = gp.dma_start(
                        out=wf3[half][:],
                        in_=wfTd[0][:, half * 2048:(half + 1) * 2048])
                L[("wfd", 0, half)] = s_wf3[half].inc(ins, 16)
            for cc in range(2):
                ins = gp.dma_start(out=wo_sb[cc % 2][:], in_=woT[cc]) if em else None
                L[("wod", cc)] = s_wo[cc % 2].inc(ins, 16)
            for cc in range(32):
                nx = cc + 1
                if nx < 32:
                    for half in range(2):
                        if em:
                            gp.wait_ge(s_v.h, L[("qsub", cc, half)])
                        ins = None
                        if em:
                            ins = gp.dma_start(
                                out=wf3[half][:],
                                in_=wfTd[nx][:, half * 2048:(half + 1) * 2048])
                        L[("wfd", nx, half)] = s_wf3[half].inc(ins, 16)
                nx = cc + 2
                if nx < 32:
                    if em:
                        gp.wait_ge(s_t.h, L[("to", cc, 1)])
                    ins = gp.dma_start(out=wo_sb[nx % 2][:], in_=woT[nx]) if em else None
                    L[("wod", nx)] = s_wo[nx % 2].inc(ins, 16)

        def vector_stream(v):
            for cc in range(32):
                for half in range(2):
                    if em:
                        v.wait_ge(s_wf3[half].h, L[("wfd", cc, half)])
                        if cc >= 2:
                            v.wait_ge(s_t.h, L[("tf", cc - 2, 1)])
                        if (cc, half) != (0, 0):
                            pq = (cc, 0) if half == 1 else (cc - 1, 1)
                            v.wait_ge(s_v.h, L[("qsub", pq[0], pq[1])])
                    la = s_v.inc(v.tensor_scalar(
                        qa_sb[:], wf3[half][:], consts[:, 2:3], None,
                        op0=ALU.is_gt) if em else None)
                    lb = s_v.inc(v.tensor_scalar(
                        qb_sb[:], wf3[half][:], consts[:, 3:4], None,
                        op0=ALU.is_lt) if em else None)
                    if em:
                        v.wait_ge(s_v.h, lb)
                    L[("qsub", cc, half)] = s_v.inc(v.tensor_sub(
                        wq3[cc % 2][:, half * 2048:(half + 1) * 2048],
                        qa_sb[:], qb_sb[:]) if em else None)
                for tc in range(2):
                    if em:
                        v.wait_ge(s_t.h, L[("tf", cc, tc)])
                        v.wait_ge(s_xt3[tc].h, L[("xtd", cc, tc)])
                        if (cc, tc) != (0, 0):
                            pf = (cc, 0) if tc == 1 else (cc - 1, 1)
                            v.wait_ge(s_v.h, L[("fin", pf[0], pf[1])])
                    L[("ffr", cc, tc)] = s_v.inc(v.scalar_tensor_tensor(
                        ffr_sb[:], fps[tc][:], consts[:, 1:2], xT_sb[tc][:],
                        op0=ALU.mult, op1=ALU.add) if em else None)
                    if em:
                        v.wait_ge(s_t.h, L[("to", cc, tc)])
                        v.wait_ge(s_v.h, L[("ffr", cc, tc)])
                        if cc > 0:
                            v.wait_ge(s_st2[tc].h, L[("st", cc - 1, tc)])
                    L[("fin", cc, tc)] = s_v.inc(v.scalar_tensor_tensor(
                        out_sb[tc][:], ops_[tc][:], obias[:, cc:cc + 1], ffr_sb[:],
                        op0=ALU.add, op1=ALU.add) if em else None)

        def tensor_stream(t):
            for cc in range(32):
                if em:
                    t.wait_ge(s_wo[cc % 2].h, L[("wod", cc)])
                for tc in range(2):
                    if em:
                        t.wait_ge(s_v.h, L[("qsub", cc, 1)])
                        if cc > 0:
                            t.wait_ge(s_v.h, L[("ffr", cc - 1, tc)])
                    last = None
                    for dc in range(32):
                        if em:
                            last = t.matmul(
                                fps[tc][:], wq3[cc % 2][:, dc * 128:(dc + 1) * 128],
                                xnTo.ap()[:, dc * 1024 + tc * 512:
                                          dc * 1024 + (tc + 1) * 512],
                                start=(dc == 0), stop=(dc == 31))
                    L[("tf", cc, tc)] = s_t.inc(last if em else None)
                    if em and cc > 0:
                        t.wait_ge(s_v.h, L[("fin", cc - 1, tc)])
                    last = None
                    for h in range(32):
                        if em:
                            last = t.matmul(
                                ops_[tc][:], wo_sb[cc % 2][:, h * 128:(h + 1) * 128],
                                attnT.ap()[:, h * 1024 + tc * 512:
                                           h * 1024 + (tc + 1) * 512],
                                start=(h == 0), stop=(h == 31))
                    L[("to", cc, tc)] = s_t.inc(last if em else None)

        return sync_stream, gpsimd_stream, vector_stream, tensor_stream

    L3 = {}

    def _snap3():
        return (s_v.v, s_s.v, s_t.v, s_st2[0].v, s_st2[1].v,
                s_wo[0].v, s_wo[1].v, s_wf3[0].v, s_wf3[1].v,
                s_xt3[0].v, s_xt3[1].v)

    def _restore3(t_):
        (s_v.v, s_s.v, s_t.v, s_st2[0].v, s_st2[1].v,
         s_wo[0].v, s_wo[1].v, s_wf3[0].v, s_wf3[1].v,
         s_xt3[0].v, s_xt3[1].v) = t_

    save = _snap3()
    sy, gp_, vv, tt_ = b3_pass(False, L3)
    gp_(None); vv(None); tt_(None); sy(None)
    endv = _snap3()
    _restore3(save)
    sy, gp_, vv, tt_ = b3_pass(True, L3)
    with nc.Block() as blk:
        blk.sync(sy)
        blk.gpsimd(gp_)
        blk.vector(vv)
        blk.tensor(tt_)
    assert _snap3() == endv

    return nc


def _storage_order(parity):
    if parity == 0:
        return [15 - i for i in range(16)]
    o = []
    for j in range(8):
        o += [14 - 2 * j, 15 - 2 * j]
    return o


_NC_CACHE = None


def kernel(L0_attn_x, L0_attn_wq_w, L0_attn_wq_b, L0_attn_wk_w, L0_attn_wk_b,
           L0_attn_wv_w, L0_attn_wv_b, L0_attn_wo_w, L0_attn_wo_b,
           L0_ffn_weight, L0_ffn_bias):
    import ml_dtypes
    global _NC_CACHE

    x = np.ascontiguousarray(np.asarray(L0_attn_x, dtype=np.float32))
    wq = np.asarray(L0_attn_wq_w, np.float32)
    wk = np.asarray(L0_attn_wk_w, np.float32)
    wv = np.asarray(L0_attn_wv_w, np.float32)
    wo = np.asarray(L0_attn_wo_w, np.float32)
    wf = np.asarray(L0_ffn_weight, np.float32)

    wqT = np.ascontiguousarray(wq.reshape(32, 128, 32, 128).transpose(0, 3, 2, 1)).reshape(32, 128, D)
    wkT = np.ascontiguousarray(
        wk.reshape(128, 32, 128).transpose(2, 1, 0)).reshape(128, D)
    wvT = np.ascontiguousarray(
        wv.reshape(128, 32, 128).transpose(2, 1, 0)).reshape(128, D)
    woT = np.ascontiguousarray(wo.reshape(32, 128, 32, 128).transpose(0, 3, 2, 1)).reshape(32, 128, D)
    wfT = np.ascontiguousarray(wf.reshape(32, 128, 32, 128).transpose(0, 3, 2, 1)).reshape(32, 128, D)
    wqb = np.ascontiguousarray(np.asarray(L0_attn_wq_b, np.float32).reshape(32, 128).T)
    wkb = np.ascontiguousarray(np.asarray(L0_attn_wk_b, np.float32).reshape(128, 1))
    wvb = np.ascontiguousarray(np.asarray(L0_attn_wv_b, np.float32).reshape(128, 1))
    wob = np.ascontiguousarray(np.asarray(L0_attn_wo_b, np.float32).reshape(32, 128).T)
    fbb = np.ascontiguousarray(np.asarray(L0_ffn_bias, np.float32).reshape(32, 128).T)
    identb = np.eye(128, dtype=np.float32).astype(ml_dtypes.bfloat16)
    identf = np.eye(128, dtype=np.float32)

    r = np.arange(128)
    tri = np.where(r[:, None] >= r[None, :], 0.0, -1.0e30).astype(np.float32)
    zeros = np.zeros((128, 128), np.float32)
    minf = np.full((128, 128), -1.0e30, np.float32)

    in_maps = []
    perms = []
    for c in range(8):
        b, parity = c // 2, c % 2
        sigma = _storage_order(parity)
        perm = np.concatenate([np.arange(g * 128, (g + 1) * 128) for g in sigma])
        perms.append(perm)
        xb = x[b]
        xtok_c = np.ascontiguousarray(xb[perm])
        own_rows = np.concatenate(
            [perm[2 * s * 128:(2 * s + 1) * 128] for s in range(8)])
        xT_c = np.ascontiguousarray(xb[own_rows].T)
        mask2_c = np.ascontiguousarray(
            np.concatenate([tri, zeros if parity == 0 else minf], axis=1))
        wfsl_c = np.ascontiguousarray(wf.reshape(8, -1)[c].reshape(128, 16384))
        in_maps.append({
            "xtok": xtok_c, "xT": xT_c, "wqT": wqT, "wkT": wkT, "wvT": wvT,
            "woT": woT, "wfT": wfT, "wfsl": wfsl_c, "wqb": wqb, "wkb": wkb,
            "wvb": wvb, "wob": wob, "fbb": fbb, "mask2": mask2_c,
            "identb": identb, "identf": identf,
        })

    if _NC_CACHE is None:
        _NC_CACHE = build_nc()
    nc = _NC_CACHE

    res = run_bass_kernel_spmd(nc, in_maps, core_ids=list(range(8)))

    out = np.empty((B, T, D), dtype=np.float32)
    for c in range(8):
        b = c // 2
        perm = perms[c]
        own_rows = np.concatenate(
            [perm[2 * s * 128:(2 * s + 1) * 128] for s in range(8)])
        out[b][own_rows] = res.results[c]["outT"].T
    return out


# version 14
# speedup vs baseline: 1.1383x; 1.1383x over previous
import numpy as np
from contextlib import ExitStack

import concourse.bass as bass
import concourse.mybir as mybir
from concourse.bass_utils import run_bass_kernel_spmd

F32 = mybir.dt.float32
BF16 = mybir.dt.bfloat16
AF = mybir.ActivationFunctionType
ALU = mybir.AluOpType
AX = mybir.AxisListType

B, T, D, H, HD = 4, 2048, 4096, 32, 128
NT = 16
ES = [16, 14, 12, 10, 8, 6, 4, 2]
RS = float(1.0 / np.sqrt(128.0))


class Sem:
    def __init__(self, nc, name):
        self.h = nc.alloc_semaphore(name)
        self.v = 0

    def inc(self, instr, n=1):
        if instr is not None:
            instr.then_inc(self.h, n)
        self.v += n
        return self.v


def pieces(width, maxw=512):
    out, off = [], 0
    while off < width:
        w = min(maxw, width - off)
        out.append((off, w))
        off += w
    return out


def slot_chunks(E):
    if E > 8:
        return [(0, 8 * 128), (8 * 128, (E - 8) * 128)]
    return [(0, E * 128)]


def build_nc():
    nc = bass.Bass("TRN2", target_bir_lowering=False)

    xtok = nc.dram_tensor("xtok", [T, D], F32, kind="ExternalInput").ap()
    xTd = nc.dram_tensor("xT", [D, 1024], F32, kind="ExternalInput").ap()
    wqT = nc.dram_tensor("wqT", [H, 128, D], F32, kind="ExternalInput").ap()
    wkT = nc.dram_tensor("wkT", [128, D], F32, kind="ExternalInput").ap()
    wvT = nc.dram_tensor("wvT", [128, D], F32, kind="ExternalInput").ap()
    woT = nc.dram_tensor("woT", [32, 128, D], F32, kind="ExternalInput").ap()
    wfTd = nc.dram_tensor("wfT", [32, 128, D], F32, kind="ExternalInput").ap()
    wfsl = nc.dram_tensor("wfsl", [128, 16384], F32, kind="ExternalInput").ap()
    wqb = nc.dram_tensor("wqb", [128, 32], F32, kind="ExternalInput").ap()
    wkb = nc.dram_tensor("wkb", [128, 1], F32, kind="ExternalInput").ap()
    wvb = nc.dram_tensor("wvb", [128, 1], F32, kind="ExternalInput").ap()
    wob = nc.dram_tensor("wob", [128, 32], F32, kind="ExternalInput").ap()
    fbb = nc.dram_tensor("fbb", [128, 32], F32, kind="ExternalInput").ap()
    mask2 = nc.dram_tensor("mask2", [128, 256], F32, kind="ExternalInput").ap()
    identb = nc.dram_tensor("identb", [128, 128], BF16, kind="ExternalInput").ap()
    identf = nc.dram_tensor("identf", [128, 128], F32, kind="ExternalInput").ap()
    outT = nc.dram_tensor("outT", [D, 1024], F32, kind="ExternalOutput").ap()

    cc_in = nc.dram_tensor("cc_in", [1, 2], F32, kind="Internal")
    cc_out = nc.dram_tensor("cc_out", [8, 2], F32, kind="Internal")

    es = ExitStack()
    es_kv = ExitStack()
    es0 = ExitStack()
    es2 = ExitStack()
    es2a = ExitStack()
    es2ps = ExitStack()
    es3 = ExitStack()

    def sbt(name, shape, dt, stack):
        return stack.enter_context(nc.sbuf_tensor(name, shape, dt))

    def pst(name, shape, dt, stack):
        return stack.enter_context(nc.psum_tensor(name, shape, dt))

    xnTo = sbt("xnTo", [128, 32 * 1024], BF16, es)
    m2 = sbt("m2", [128, 256], F32, es)
    idb = sbt("idb", [128, 128], BF16, es)
    idf = sbt("idf", [128, 128], F32, es)
    qb_s = sbt("qb_s", [128, 32], F32, es)
    qb2 = sbt("qb2", [128, 32], F32, es)
    kb_s = sbt("kb_s", [128, 1], F32, es)
    vb_s = sbt("vb_s", [128, 1], F32, es)
    ob_s = sbt("ob_s", [128, 32], F32, es)
    fb_s = sbt("fb_s", [128, 32], F32, es)
    obias = sbt("obias", [128, 32], F32, es)
    ones_c = sbt("ones_c", [128, 1], F32, es)
    ones8 = sbt("ones8", [8, 1], F32, es)
    ones_r = sbt("ones_r", [1, 128], F32, es)
    ones_cb = sbt("ones_cb", [128, 1], BF16, es)
    consts = sbt("consts", [128, 4], F32, es)
    absrows = sbt("absrows", [128, 16], F32, es)
    wfabs = sbt("wfabs", [128, 8], F32, es)
    cols2 = sbt("cols2", [128, 2], F32, es)
    tot_sb = sbt("tot_sb", [1, 2], F32, es)
    g_sb = sbt("g_sb", [8, 2], F32, es)
    row4 = sbt("row4", [1, 4], F32, es)

    kTb = sbt("kTb", [128, T], BF16, es_kv)
    vtok = sbt("vtok", [128, NT * 128], BF16, es_kv)

    xnTo_r = xnTo.ap().rearrange("p (dc t) -> p dc t", dc=32)

    s_ldx = [Sem(nc, "s_ldx0"), Sem(nc, "s_ldx1")]
    s_wfp = [Sem(nc, "s_wfp0"), Sem(nc, "s_wfp1")]
    s_wq = [Sem(nc, "s_wq0"), Sem(nc, "s_wq1")]
    s_wo = [Sem(nc, "s_wo0"), Sem(nc, "s_wo1")]
    s_wf3 = [Sem(nc, "s_wf30"), Sem(nc, "s_wf31")]
    s_xt3 = [Sem(nc, "s_xt30"), Sem(nc, "s_xt31")]
    s_st2 = [Sem(nc, "s_st20"), Sem(nc, "s_st21")]
    s_ldf = Sem(nc, "s_ldf")
    s_ldc = Sem(nc, "s_ldc")
    s_ldw = Sem(nc, "s_ldw")
    s_v = Sem(nc, "s_v")
    s_s = Sem(nc, "s_s")
    s_t = Sem(nc, "s_t")
    s_cc = Sem(nc, "s_cc")
    s_st = Sem(nc, "s_st")
    s_io = Sem(nc, "s_io")

    xt = [sbt(f"xt{i}", [128, D], F32, es0) for i in range(2)]
    xnt = [sbt(f"xnt{i}", [128, D], BF16, es0) for i in range(2)]
    xnf = [sbt(f"xnf{i}", [128, D], BF16, es0) for i in range(2)]
    wf0 = [sbt(f"wf0_{i}", [128, 2048], F32, es0) for i in range(2)]
    wk_sb = sbt("wk_sb", [128, D], BF16, es0)
    wv_sb = sbt("wv_sb", [128, D], BF16, es0)
    vtt = sbt("vtt", [128, 128], BF16, es0)
    bnst = sbt("bnst", [128, 48], F32, es0)
    stat = sbt("stat", [128, 8], F32, es0)
    dump = sbt("dump", [128, D], BF16, es0)
    kvK = pst("kvK", [128, 128], F32, es0)
    kvV = pst("kvV", [128, 128], F32, es0)
    tps = [pst(f"tps{i}", [128, 512], BF16, es0) for i in range(2)]
    vtp = pst("vtp", [128, 128], BF16, es0)
    tot_ps = pst("tot_ps", [1, 2], F32, es0)
    tot2_ps = pst("tot2_ps", [1, 2], F32, es0)
    cons_ps = pst("cons_ps", [128, 4], F32, es0)

    with nc.Block() as blk:
        @blk.sync
        def _(s):
            for dst, src in ((m2, mask2), (idb, identb), (idf, identf),
                             (qb_s, wqb), (kb_s, wkb), (vb_s, wvb),
                             (ob_s, wob), (fb_s, fbb)):
                s_ldc.inc(s.dma_start(out=dst[:], in_=src), 16)
            for tt in range(2):
                s_ldx[tt % 2].inc(s.dma_start(
                    out=xt[tt][:],
                    in_=xtok[tt * 128:(tt + 1) * 128, :]), 16)
            for ci in range(2):
                s_wfp[ci % 2].inc(s.dma_start(
                    out=wf0[ci][:],
                    in_=wfsl[:, ci * 2048:(ci + 1) * 2048]), 16)

        @blk.gpsimd
        def _(g):
            s_ldw.inc(g.dma_start(out=wk_sb[:], in_=wkT), 16)
            s_ldw.inc(g.dma_start(out=wv_sb[:], in_=wvT), 16)

        @blk.vector
        def _(v):
            v.memset(ones_c[:], 1.0)
            v.memset(ones_cb[:], 1.0)
            v.memset(ones8[:], 1.0)
            v.memset(ones_r[:], 1.0)

    ldc_all = s_ldc.v
    ldw_kv = s_ldw.v

    LP = {}

    def nwf(tt):
        return 1 if tt < 8 else 0

    def p0_pass(em):
        def sync_stream(s):
            for tt in range(NT):
                if tt + 2 < NT:
                    if em:
                        s.wait_ge(s_s.h, LP[("xn", tt)])
                    ins = None
                    if em:
                        ins = s.dma_start(
                            out=xt[tt % 2][:],
                            in_=xtok[(tt + 2) * 128:(tt + 3) * 128, :])
                    s_ldx[tt % 2].inc(ins, 16)
                ci = tt + 2
                if 2 <= ci < 8:
                    if em:
                        s.wait_ge(s_s.h, LP[("wfa", tt)])
                    ins = None
                    if em:
                        ins = s.dma_start(
                            out=wf0[ci % 2][:],
                            in_=wfsl[:, ci * 2048:(ci + 1) * 2048])
                    s_wfp[ci % 2].inc(ins, 16)

        def vector_stream(v):
            for it in range(NT + 2):
                if it < NT:
                    tt = it
                    p = tt % 2
                    if em:
                        v.wait_ge(s_ldx[p].h, 16 * (tt // 2 + 1))
                        if tt == 0:
                            v.wait_ge(s_ldc.h, ldc_all)
                        if tt >= 1:
                            v.wait_ge(s_v.h, LP[("aggr", tt - 1)])
                        if tt >= 1:
                            v.wait_ge(s_s.h, LP[("ln", tt - 1)])
                    last = None
                    for i in range(8):
                        if em:
                            last = v.bn_stats(bnst[:, i * 6:(i + 1) * 6],
                                              xt[p][:, i * 512:(i + 1) * 512])
                    lb = s_v.inc(last if em else None)
                    if em:
                        v.wait_ge(s_v.h, lb)
                    LP[("aggr", tt)] = s_v.inc(
                        v.bn_aggr(stat[:, 0:2], bnst[:]) if em else None)
                    if em:
                        v.wait_ge(s_v.h, LP[("aggr", tt)])
                    LP[("veps", tt)] = s_v.inc(
                        v.tensor_scalar(stat[:, 5:6], stat[:, 1:2], 1.0e-5,
                                        None, op0=ALU.add) if em else None)
                    if em:
                        v.wait_ge(s_s.h, LP[("rstd", tt)])
                        if tt >= 1:
                            v.wait_ge(s_s.h, LP[("xn", tt - 1)])
                    LP[("nmr", tt)] = s_v.inc(
                        v.scalar_tensor_tensor(
                            stat[:, 3:4], stat[:, 0:1], -1.0, stat[:, 2:3],
                            op0=ALU.mult, op1=ALU.mult) if em else None)
                if 1 <= it <= NT:
                    tt = it - 1
                    own = (tt % 2 == 0)
                    slot = tt // 2
                    p = tt % 2
                    for qd in range(8):
                        if em:
                            v.wait_ge(s_t.h, LP[("tp", tt, qd)])
                        dc0 = qd * 4
                        src_ = tps[qd % 2].ap().rearrange("p (a b) -> p a b", a=4)
                        ins = None
                        if own:
                            if em:
                                ins = v.tensor_copy(
                                    xnTo_r[:, dc0:dc0 + 4,
                                           slot * 128:(slot + 1) * 128], src_)
                        else:
                            if em:
                                if qd == 0 and tt >= 2:
                                    v.wait_ge(s_t.h, LP[("kvV", tt - 2)])
                                ins = v.tensor_copy(
                                    xnf[p].ap().rearrange(
                                        "p (a b) -> p a b",
                                        a=32)[:, dc0:dc0 + 4, :], src_)
                        LP[("ev", tt, qd)] = s_v.inc(ins)
                if 2 <= it:
                    tt = it - 2
                    if em:
                        v.wait_ge(s_t.h, LP[("vtp", tt)])
                    LP[("vte", tt)] = s_v.inc(
                        v.tensor_copy(vtok[:, tt * 128:(tt + 1) * 128], vtp[:])
                        if em else None)

        def scalar_stream(s_):
            for it in range(NT + 2):
                if it < NT:
                    tt = it
                    p = tt % 2
                    if em:
                        s_.wait_ge(s_v.h, LP[("veps", tt)])
                    LP[("ln", tt)] = s_s.inc(
                        s_.activation(stat[:, 4:5], stat[:, 5:6], AF.Ln,
                                      bias=0.0, scale=1.0) if em else None)
                    if em:
                        s_.wait_ge(s_s.h, LP[("ln", tt)])
                    LP[("rstd", tt)] = s_s.inc(
                        s_.activation(stat[:, 2:3], stat[:, 4:5], AF.Exp,
                                      bias=0.0, scale=-0.5) if em else None)
                    if em:
                        s_.wait_ge(s_v.h, LP[("nmr", tt)])
                        s_.wait_ge(s_s.h, LP[("rstd", tt)])
                        if tt >= 2:
                            s_.wait_ge(s_t.h, LP[("tp", tt - 2, 7)])
                    LP[("xn", tt)] = s_s.inc(
                        s_.activation(xnt[p][:], xt[p][:], AF.Identity,
                                      bias=stat[:, 3:4], scale=stat[:, 2:3])
                        if em else None)
                if 1 <= it <= NT:
                    tt = it - 1
                    p = tt % 2
                    if em:
                        s_.wait_ge(s_s.h, LP[("xn", tt)])
                        if tt >= 1:
                            s_.wait_ge(s_s.h, LP[("dmp", tt - 1)])
                    LP[("abs", tt)] = s_s.inc(
                        s_.activation(dump[:], xnt[p][:], AF.Abs,
                                      accum_out=absrows[:, tt:tt + 1])
                        if em else None)
                    LP[("dmp", tt)] = LP[("abs", tt)]
                    if nwf(tt):
                        if em:
                            s_.wait_ge(s_wfp[p].h, 16 * (tt // 2 + 1))
                            s_.wait_ge(s_s.h, LP[("abs", tt)])
                        LP[("wfa", tt)] = s_s.inc(
                            s_.activation(dump[:, 0:2048], wf0[p][:], AF.Abs,
                                          accum_out=wfabs[:, tt:tt + 1])
                            if em else None)
                        LP[("dmp", tt)] = LP[("wfa", tt)]
                    else:
                        LP[("wfa", tt)] = LP[("abs", tt)]
                if 2 <= it:
                    tt = it - 2
                    if em:
                        s_.wait_ge(s_t.h, LP[("kvK", tt)])
                    LP[("ke", tt)] = s_s.inc(
                        s_.activation(kTb[:, tt * 128:(tt + 1) * 128], kvK[:],
                                      AF.Identity, bias=kb_s[:]) if em else None)
                    if em:
                        s_.wait_ge(s_t.h, LP[("kvV", tt)])
                        if tt >= 1:
                            s_.wait_ge(s_t.h, LP[("vtp", tt - 1)])
                        s_.wait_ge(s_s.h, LP[("ke", tt)])
                    LP[("ve", tt)] = s_s.inc(
                        s_.activation(vtt[:], kvV[:], AF.Identity, bias=vb_s[:])
                        if em else None)

        def tensor_stream(t):
            for it in range(NT + 1):
                if it < NT:
                    tt = it
                    p = tt % 2
                    if em:
                        t.wait_ge(s_s.h, LP[("xn", tt)])
                        if tt == 0:
                            t.wait_ge(s_ldc.h, ldc_all)
                    for qd in range(8):
                        if em:
                            gq = 8 * tt + qd
                            if gq >= 2:
                                pt_, pq = (gq - 2) // 8, (gq - 2) % 8
                                t.wait_ge(s_v.h, LP[("ev", pt_, pq)])
                        last = None
                        for i in range(4):
                            dc = qd * 4 + i
                            if em:
                                last = t.transpose(
                                    tps[qd % 2][:, i * 128:(i + 1) * 128],
                                    xnt[p][:, dc * 128:(dc + 1) * 128], idb[:])
                        LP[("tp", tt, qd)] = s_t.inc(last if em else None)
                if it >= 1:
                    tt = it - 1
                    own = (tt % 2 == 0)
                    slot = tt // 2
                    p = tt % 2
                    if em:
                        t.wait_ge(s_v.h, LP[("ev", tt, 7)])
                        if tt == 0:
                            t.wait_ge(s_ldw.h, ldw_kv)
                        if tt >= 1:
                            t.wait_ge(s_s.h, LP[("ve", tt - 1)])
                    for wi, ps in ((0, kvK), (1, kvV)):
                        w = wk_sb if wi == 0 else wv_sb
                        last = None
                        for dc in range(32):
                            if em:
                                if own:
                                    rhs = xnTo.ap()[:, dc * 1024 + slot * 128:
                                                    dc * 1024 + (slot + 1) * 128]
                                else:
                                    rhs = xnf[p].ap()[:, dc * 128:(dc + 1) * 128]
                                last = t.matmul(ps[:],
                                                w[:, dc * 128:(dc + 1) * 128],
                                                rhs, start=(dc == 0),
                                                stop=(dc == 31))
                        LP[("kvK" if wi == 0 else "kvV", tt)] = s_t.inc(
                            last if em else None)
                    if em:
                        t.wait_ge(s_s.h, LP[("ve", tt)])
                        if tt >= 1:
                            t.wait_ge(s_v.h, LP[("vte", tt - 1)])
                    LP[("vtp", tt)] = s_t.inc(
                        t.transpose(vtp[:], vtt[:], idb[:]) if em else None)

        return sync_stream, vector_stream, scalar_stream, tensor_stream

    def _snap0():
        return (s_v.v, s_s.v, s_t.v, s_ldx[0].v, s_ldx[1].v,
                s_wfp[0].v, s_wfp[1].v)

    def _restore0(t_):
        (s_v.v, s_s.v, s_t.v, s_ldx[0].v, s_ldx[1].v,
         s_wfp[0].v, s_wfp[1].v) = t_

    save0 = _snap0()
    sy0, vv0, ss0, tt0 = p0_pass(False)
    sy0(None); vv0(None); ss0(None); tt0(None)
    endv0 = _snap0()
    _restore0(save0)
    sy0, vv0, ss0, tt0 = p0_pass(True)
    with nc.Block() as blk:
        blk.sync(sy0)
        blk.vector(vv0)
        blk.scalar(ss0)
        blk.tensor(tt0)
    assert _snap0() == endv0

    with nc.Block() as blk:
        @blk.vector
        def _(v):
            s_v.inc(v.reduce_sum(cols2[:, 0:1], absrows[:], axis=AX.X))
            s_v.inc(v.reduce_sum(cols2[:, 1:2], wfabs[:], axis=AX.X))
        v_cols = s_v.v

        @blk.tensor
        def _(t):
            t.wait_ge(s_v.h, v_cols)
            s_t.inc(t.matmul(tot_ps[:], ones_c[:], cols2[:], start=True, stop=True))
        t_tot = s_t.v

        @blk.vector
        def _(v):
            v.wait_ge(s_t.h, t_tot)
            s_v.inc(v.tensor_copy(tot_sb[:], tot_ps[:]))
        v_tot = s_v.v

        @blk.gpsimd
        def _(g):
            g.wait_ge(s_v.h, v_tot)
            s_io.inc(g.dma_start(out=cc_in.ap(), in_=tot_sb[:]), 16)
            g.wait_ge(s_io.h, s_io.v)
            s_cc.inc(g.collective_compute(
                "AllGather", ALU.bypass, replica_groups=[list(range(8))],
                ins=[cc_in.ap().opt()], outs=[cc_out.ap().opt()]))
            g.wait_ge(s_cc.h, s_cc.v)
            s_io.inc(g.dma_start(out=g_sb[:], in_=cc_out.ap()), 16)
        io_g = s_io.v

        @blk.tensor
        def _(t):
            t.wait_ge(s_io.h, io_g)
            s_t.inc(t.matmul(tot2_ps[:], ones8[:], g_sb[:], start=True, stop=True))
        t_tot2 = s_t.v

        @blk.vector
        def _(v):
            v.wait_ge(s_t.h, t_tot2)
            s_v.inc(v.tensor_scalar(row4[0:1, 0:1], tot2_ps[0:1, 0:1],
                                    1.0 / (2.0 * B * T * D), 1.0e-8,
                                    op0=ALU.mult, op1=ALU.max))
            b0 = s_v.inc(v.tensor_scalar(row4[0:1, 1:2], tot2_ps[0:1, 1:2],
                                         1.0 / (float(D) * D), 1.0e-8,
                                         op0=ALU.mult, op1=ALU.max))
            v.wait_ge(s_v.h, b0)
            b1 = s_v.inc(v.tensor_scalar(row4[0:1, 2:3], row4[0:1, 1:2], 0.5,
                                         None, op0=ALU.mult))
            v.wait_ge(s_v.h, b1)
            s_v.inc(v.tensor_scalar(row4[0:1, 3:4], row4[0:1, 2:3], -1.0,
                                    None, op0=ALU.mult))
        v_row4 = s_v.v

        @blk.tensor
        def _(t):
            t.wait_ge(s_v.h, v_row4)
            s_t.inc(t.matmul(cons_ps[:], ones_r[:], row4[:], start=True, stop=True))
        t_cons = s_t.v

        @blk.vector
        def _(v):
            v.wait_ge(s_t.h, t_cons)
            b2 = s_v.inc(v.tensor_copy(consts[:], cons_ps[:]))
            v.wait_ge(s_v.h, b2)
            s_v.inc(v.scalar_tensor_tensor(obias[:], fb_s[:], consts[:, 0:1],
                                           ob_s[:], op0=ALU.mult, op1=ALU.add))
            s_v.inc(v.tensor_scalar(qb2[:], qb_s[:], RS, None, op0=ALU.mult))

    es0.close()

    attnT = sbt("attnT", [128, 32 * 1024], BF16, es2)
    qTt = [sbt(f"qTt{i}", [128, 4 * 1024], BF16, es2a) for i in range(2)]
    wq_sb = [sbt(f"wq_sb{i}", [128, D], BF16, es2a) for i in range(2)]
    Pt = [sbt("Pt0", [128, 4 * 2048], BF16, es2a),
          sbt("Pt1", [128, 4 * 1792], BF16, es2a)]
    PS = [2048, 1792]
    sqb = Pt[1].ap()[:, 0:512]
    da_b = sbt("da_b", [128, 8], F32, es2a)
    db_b = sbt("db_b", [128, 8], F32, es2a)
    dsum = sbt("dsum", [128, 4], F32, es2a)
    rd8 = sbt("rd8", [128, 8], F32, es2a)
    qmx = sbt("qmx", [1, 8], F32, es2a)
    qm4 = sbt("qm4", [1, 4], F32, es2a)
    kmx = sbt("kmx", [1, 8], F32, es2a)
    c4r = sbt("c4r", [1, 4], F32, es2a)
    negc = sbt("negc", [128, 4], F32, es2a)
    msk_sb = [sbt(f"msk_sb{i}", [128, 256], F32, es2a) for i in range(2)]
    ptm_sb = [sbt(f"ptm_sb{i}", [128, 1024], BF16, es2a) for i in range(2)]


    sAB = [pst(f"sps{i}", [128, 1024], F32, es2ps) for i in range(2)]
    ptAB = [pst(f"ptps{i}", [128, 1024], BF16, es2ps) for i in range(2)]
    avq = pst("avq", [128, 512], F32, es2ps)
    qps = pst("qps", [128, 512], F32, es2ps)
    attnT_r = attnT.ap().rearrange("p (h t) -> p h t", h=32)

    gstate = {
        "sbank_free": [(s_s, 0), (s_s, 0)],
        "ptbank_free": [0, 0],
        "msk_free": [0, 0],
        "av_free": 0,
        "qps_free": (s_s, 0),
    }
    LBL = {}

    def q_unit(t, em, g, u):
        h, tc = u // 2, u % 2
        hg = 4 * g + h
        if em:
            t.wait_ge(s_wq[hg % 2].h, 16 * (hg // 2 + 1))
            qsem, qval = gstate["qps_free"]
            t.wait_ge(qsem.h, qval)
        last = None
        for dc in range(32):
            if em:
                last = t.matmul(
                    qps[:], wq_sb[hg % 2][:, dc * 128:(dc + 1) * 128],
                    xnTo.ap()[:, dc * 1024 + tc * 512:dc * 1024 + (tc + 1) * 512],
                    start=(dc == 0), stop=(dc == 31))
        LBL[("qmm", g, h, tc)] = s_t.inc(last if em else None)
        if em:
            gstate["qps_free"] = (s_s, LBL[("qev", g, h, tc)])

    def qev_unit(s_e, em, g, u):
        h, tc = u // 2, u % 2
        hg = 4 * g + h
        if em:
            s_e.wait_ge(s_t.h, LBL[("qmm", g, h, tc)])
        ins = None
        if em:
            ins = s_e.activation(
                qTt[g % 2][:, h * 1024 + tc * 512:h * 1024 + (tc + 1) * 512],
                qps[:], AF.Identity, bias=qb2[:, hg:hg + 1], scale=RS)
        LBL[("qev", g, h, tc)] = s_s.inc(ins)

    def group_pass(b, em):
        G = b

        def norms_prologue_T(t):
            if b == 0 and em:
                t.wait_ge(s_s.h, LBL[("qev", 0, 3, 1)])
            if b == 0:
                for j in range(4):
                    if em:
                        t.wait_ge(s_v.h, LBL[("ksq", j)])
                        if j > 0:
                            t.wait_ge(s_v.h, LBL[("krm", j - 1)])
                    ins = None
                    if em:
                        ins = t.matmul(qps[0:1, 0:512], ones_cb[:], sqb,
                                       start=True, stop=True)
                    LBL[("knm", j)] = s_t.inc(ins)
            for p in range(8):
                if em:
                    t.wait_ge(s_v.h, LBL[("qsq", G, p)])
                    if p > 0:
                        t.wait_ge(s_v.h, LBL[("qrm", G, p - 1)])
                    elif b == 0:
                        t.wait_ge(s_v.h, LBL[("krm", 3)])
                ins = None
                if em:
                    ins = t.matmul(qps[0:1, 0:512], ones_cb[:], sqb,
                                   start=True, stop=True)
                LBL[("qnm", G, p)] = s_t.inc(ins)
            if em:
                t.wait_ge(s_v.h, LBL[("ncrow", G)])
            ins = None
            if em:
                ins = t.matmul(avq[:, 0:4], ones_r[:], c4r[0:1, :],
                               start=True, stop=True)
            LBL[("ncrep", G)] = s_t.inc(ins)

        def tensor_stream(t):
            if b == 0:
                for u in range(8):
                    q_unit(t, em, 0, u)
            norms_prologue_T(t)
            for s in range(9):
                if s == 8:
                    emit_ptav_rounds(t, 7, 0, ES[7] // 2)
                    break
                if b < 7:
                    if s == 0 and em:
                        t.wait_ge(s_v.h, LBL[("qrm", G, 7)])
                    q_unit(t, em, b + 1, s)
                E = ES[s]
                ch = slot_chunks(E)
                nch = len(ch)
                steps = [(h, ci) for h in range(4) for ci in range(nch)]
                nr_prev = ES[s - 1] // 2 if s >= 1 else 0
                rdone = 0
                for idx, (h, ci) in enumerate(steps):
                    bb = LBL[("bank", G, s, idx)]
                    if em:
                        sem_, val_ = gstate["sbank_free"][bb]
                        t.wait_ge(sem_.h, val_)
                        t.wait_ge(s_s.h, LBL[("qev", G, h, 1)])
                    off0, cw = ch[ci]
                    last = None
                    for (off, w) in pieces(cw):
                        if em:
                            last = t.matmul(
                                sAB[bb][:, off:off + w],
                                qTt[G % 2][:, h * 1024 + s * 128:
                                           h * 1024 + (s + 1) * 128],
                                kTb[:, (16 - E) * 128 + off0 + off:
                                    (16 - E) * 128 + off0 + off + w],
                                start=True, stop=True)
                    LBL[("sc", G, s, idx)] = s_t.inc(last if em else None)
                    if em:
                        gstate["sbank_free"][bb] = (s_s, LBL[("exp", G, s, idx)])
                    if s >= 1 and idx >= 1:
                        take = min(1 + rdone, nr_prev) - rdone
                        if take > 0:
                            emit_ptav_rounds(t, s - 1, rdone, rdone + take)
                            rdone += take
                if s >= 1 and rdone < nr_prev:
                    emit_ptav_rounds(t, s - 1, rdone, nr_prev)

        def emit_ptav_rounds(t, s, r0, r1):
            E = ES[s]
            nr = E // 2
            for r in range(r0, r1):
                pb = LBL[("ptb", G, s, r)]
                ps_ = PS[s % 2]
                csem = s_v if pb == 0 else s_s
                if em:
                    t.wait_ge(csem.h, gstate["ptbank_free"][pb])
                    ci_hi = 0 if (2 * r + 1) < 8 else 1
                    t.wait_ge(s_v.h, LBL[("pn", G, s, ci_hi)])
                last = None
                for j in range(2):
                    kk = 2 * r + j
                    for h in range(4):
                        if em:
                            last = t.transpose(
                                ptAB[pb][:, j * 512 + h * 128:
                                         j * 512 + (h + 1) * 128],
                                Pt[s % 2][:, h * ps_ + kk * 128:
                                          h * ps_ + (kk + 1) * 128],
                                idb[:])
                LBL[("pt", G, s, r)] = s_t.inc(last if em else None)
                if em:
                    gstate["ptbank_free"][pb] = LBL[("ptc", G, s, r)]
                if r >= 1:
                    emit_av_round(t, s, r - 1)
            if r1 == nr:
                emit_av_round(t, s, nr - 1)
                if em:
                    gstate["av_free"] = LBL[("avev", G, s)]

        def emit_av_round(t, s, r):
            E = ES[s]
            nr = E // 2
            pb = LBL[("ptb", G, s, r)]
            csem = s_v if pb == 0 else s_s
            if em:
                t.wait_ge(csem.h, LBL[("ptc", G, s, r)])
                if r == 0:
                    t.wait_ge(s_v.h, gstate["av_free"])
            for j in range(2):
                kk = 2 * r + j
                kabs = 16 - E + kk
                ins = None
                if em:
                    ins = t.matmul(avq[:],
                                   vtok[:, kabs * 128:(kabs + 1) * 128],
                                   ptm_sb[pb][:, j * 512:(j + 1) * 512],
                                   start=(kk == 0), stop=(kk == E - 1))
                LBL[("av", G, s)] = s_t.inc(ins)

        def emit_ptc(eng, s, want_pb, r0=0, r1=None):
            E = ES[s]
            nr = E // 2
            if r1 is None:
                r1 = nr
            for r in range(r0, min(r1, nr)):
                pb = LBL[("ptb", G, s, r)]
                if pb != want_pb:
                    continue
                if em:
                    eng.wait_ge(s_t.h, LBL[("pt", G, s, r)])
                ins = None
                if em:
                    if want_pb == 1:
                        ins = eng.copy(ptm_sb[pb][:], ptAB[pb][:])
                    else:
                        ins = eng.tensor_copy(ptm_sb[pb][:], ptAB[pb][:])
                sem = s_s if want_pb == 1 else s_v
                LBL[("ptc", G, s, r)] = sem.inc(ins)

        def scalar_stream(s_e):
            if b == 0:
                for u in range(8):
                    qev_unit(s_e, em, 0, u)
            if em:
                s_e.wait_ge(s_v.h, LBL[("qm4", G)])
            ins = s_e.activation(c4r[0:1, :], qm4[0:1, :], AF.Ln) if em else None
            l_ln = s_s.inc(ins)
            if em:
                s_e.wait_ge(s_s.h, l_ln)
            ins = None
            if em:
                ins = s_e.activation(c4r[0:1, :], c4r[0:1, :], AF.Exp, scale=0.5)
            LBL[("c4", G)] = s_s.inc(ins)
            for s in range(8):
                E = ES[s]
                ch = slot_chunks(E)
                nch = len(ch)
                steps = [(h, ci) for h in range(4) for ci in range(nch)]
                sr_done = 0
                for idx, (h, ci) in enumerate(steps):
                    bu = LBL[("bank", G, s, idx)]
                    ps_ = PS[s % 2]
                    if em:
                        if ci == 0:
                            s_e.wait_ge(s_v.h, LBL[("msk", G, s, idx)])
                        else:
                            s_e.wait_ge(s_t.h, LBL[("sc", G, s, idx)])
                        if s == 0 and idx < 2:
                            s_e.wait_ge(s_v.h, LBL[("ncev", G)])
                    off0, cw = ch[ci]
                    col = ci * 4 + h
                    last = None
                    if ci == 0:
                        if em:
                            last = s_e.activation(
                                Pt[s % 2][:, h * ps_:h * ps_ + 256],
                                msk_sb[h % 2][:], AF.Exp,
                                bias=negc[:, h:h + 1], scale=1.0,
                                accum_out=da_b[:, col:col + 1])
                        if cw > 256:
                            if em:
                                last = s_e.activation(
                                    Pt[s % 2][:, h * ps_ + 256:h * ps_ + cw],
                                    sAB[bu][:, 256:cw], AF.Exp,
                                    bias=negc[:, h:h + 1], scale=1.0,
                                    accum_out=db_b[:, col:col + 1])
                    else:
                        if em:
                            last = s_e.activation(
                                Pt[s % 2][:, h * ps_ + 1024:h * ps_ + 1024 + cw],
                                sAB[bu][:, 0:cw], AF.Exp,
                                bias=negc[:, h:h + 1], scale=1.0,
                                accum_out=da_b[:, col:col + 1])
                    LBL[("exp", G, s, idx)] = s_s.inc(last if em else None)
                    if s >= 1:
                        emit_ptc(s_e, s - 1, 1, sr_done, idx)
                        sr_done = max(sr_done, min(idx, ES[s - 1] // 2))
                if s >= 1:
                    emit_ptc(s_e, s - 1, 1, sr_done)
                if b < 7:
                    qev_unit(s_e, em, b + 1, s)
            emit_ptc(s_e, 7, 1)

        def emit_avev(v, s):
            if em:
                v.wait_ge(s_t.h, LBL[("av", G, s)])
            ins = None
            if em:
                ins = v.tensor_copy(
                    attnT_r[:, 4 * G:4 * G + 4, s * 128:(s + 1) * 128],
                    avq.ap().rearrange("p (a b) -> p a b", a=4))
            LBL[("avev", G, s)] = s_v.inc(ins)

        def vector_stream(v):
            if b == 0:
                for j in range(4):
                    if em:
                        if j > 0:
                            v.wait_ge(s_t.h, LBL[("knm", j - 1)])
                    ins = None
                    if em:
                        ins = v.tensor_mul(sqb, kTb[:, j * 512:(j + 1) * 512],
                                           kTb[:, j * 512:(j + 1) * 512])
                    LBL[("ksq", j)] = s_v.inc(ins)
                    if em:
                        v.wait_ge(s_t.h, LBL[("knm", j)])
                    ins = None
                    if em:
                        ins = v.reduce_max(kmx[0:1, j:j + 1], qps[0:1, 0:512],
                                           axis=AX.X)
                    LBL[("krm", j)] = s_v.inc(ins)
                if em:
                    v.wait_ge(s_v.h, LBL[("krm", 3)])
                ins = None
                if em:
                    ins = v.reduce_max(kmx[0:1, 4:5], kmx[0:1, 0:4], axis=AX.X)
                LBL[("kmax", 0)] = s_v.inc(ins)
            for p in range(8):
                h, j = p % 4, p // 4
                if em:
                    if b == 0:
                        v.wait_ge(s_s.h, LBL[("qev", G, h, 1)])
                    if p == 0 and b == 0:
                        v.wait_ge(s_t.h, LBL[("knm", 3)])
                    if p > 0:
                        v.wait_ge(s_t.h, LBL[("qnm", G, p - 1)])
                ins = None
                if em:
                    ins = v.tensor_mul(
                        sqb,
                        qTt[G % 2][:, h * 1024 + j * 512:h * 1024 + (j + 1) * 512],
                        qTt[G % 2][:, h * 1024 + j * 512:h * 1024 + (j + 1) * 512])
                LBL[("qsq", G, p)] = s_v.inc(ins)
                if em:
                    v.wait_ge(s_t.h, LBL[("qnm", G, p)])
                ins = None
                if em:
                    ins = v.reduce_max(qmx[0:1, p:p + 1], qps[0:1, 0:512],
                                       axis=AX.X)
                LBL[("qrm", G, p)] = s_v.inc(ins)
            if em:
                v.wait_ge(s_v.h, LBL[("qrm", G, 7)])
            ins = None
            if em:
                ins = v.tensor_tensor(qm4[0:1, :], qmx[0:1, 0:4],
                                      qmx[0:1, 4:8], op=ALU.max)
            l_qm = s_v.inc(ins)
            if em:
                v.wait_ge(s_v.h, l_qm)
                v.wait_ge(s_v.h, LBL[("kmax", 0)])
            ins = None
            if em:
                ins = v.tensor_scalar(qm4[0:1, :], qm4[0:1, :], kmx[0:1, 4:5],
                                      None, op0=ALU.mult)
            LBL[("qm4", G)] = s_v.inc(ins)
            if em:
                v.wait_ge(s_s.h, LBL[("c4", G)])
            ins = None
            if em:
                ins = v.tensor_scalar(c4r[0:1, :], c4r[0:1, :],
                                      -1.05 * RS, None, op0=ALU.mult)
            LBL[("ncrow", G)] = s_v.inc(ins)
            if em:
                v.wait_ge(s_t.h, LBL[("ncrep", G)])
            ins = v.tensor_copy(negc[:], avq[:, 0:4]) if em else None
            LBL[("ncev", G)] = s_v.inc(ins)
            if em and gstate["av_free"] < LBL[("ncev", G)]:
                gstate["av_free"] = LBL[("ncev", G)]
            for s in range(8):
                E = ES[s]
                ch = slot_chunks(E)
                nch = len(ch)
                steps = [(h, ci) for h in range(4) for ci in range(nch)]
                second0 = ch[0][1] > 256
                if second0:
                    s_v.inc(v.memset(db_b[:], 0.0) if em else None)
                vr_done = 0
                for idx, (h, ci) in enumerate(steps):
                    bb = LBL[("bank", G, s, idx)]
                    if ci != 0:
                        continue
                    if em:
                        v.wait_ge(s_t.h, LBL[("sc", G, s, idx)])
                        v.wait_ge(s_s.h, gstate["msk_free"][h % 2])
                    ins = None
                    if em:
                        ins = v.tensor_add(msk_sb[h % 2][:],
                                           sAB[bb][:, 0:256], m2[:])
                    LBL[("msk", G, s, idx)] = s_v.inc(ins)
                    if em:
                        gstate["msk_free"][h % 2] = LBL[("exp", G, s, idx)]
                    if s >= 1:
                        emit_ptc(v, s - 1, 0, vr_done, idx + 1)
                        vr_done = max(vr_done, min(idx + 1, ES[s - 1] // 2))
                if s >= 1:
                    emit_ptc(v, s - 1, 0, vr_done)
                    emit_avev(v, s - 1)
                if em:
                    v.wait_ge(s_s.h, LBL[("exp", G, s, len(steps) - 1)])
                if nch == 2:
                    l1 = s_v.inc(v.tensor_add(dsum[:], da_b[:, 0:4], db_b[:, 0:4])
                                 if em else None)
                    if em:
                        v.wait_ge(s_v.h, l1)
                    l2 = s_v.inc(v.tensor_add(dsum[:], dsum[:], da_b[:, 4:8])
                                 if em else None)
                    if em:
                        v.wait_ge(s_v.h, l2)
                    LBL[("rd8", G, s)] = s_v.inc(
                        v.reciprocal(rd8[:, 0:4], dsum[:]) if em else None)
                elif second0:
                    l1 = s_v.inc(v.tensor_add(dsum[:], da_b[:, 0:4],
                                              db_b[:, 0:4]) if em else None)
                    if em:
                        v.wait_ge(s_v.h, l1)
                    LBL[("rd8", G, s)] = s_v.inc(
                        v.reciprocal(rd8[:, 0:4], dsum[:]) if em else None)
                else:
                    LBL[("rd8", G, s)] = s_v.inc(
                        v.reciprocal(rd8[:, 0:4], da_b[:, 0:4]) if em else None)
                if em:
                    v.wait_ge(s_v.h, LBL[("rd8", G, s)])
                for ci in range(nch):
                    cw = ch[ci][1]
                    ps_ = PS[s % 2]
                    last = None
                    for h in range(4):
                        if em:
                            last = v.tensor_scalar(
                                Pt[s % 2][:, h * ps_ + ci * 1024:
                                          h * ps_ + ci * 1024 + cw],
                                Pt[s % 2][:, h * ps_ + ci * 1024:
                                          h * ps_ + ci * 1024 + cw],
                                rd8[:, h:h + 1], None,
                                op0=ALU.mult)
                    LBL[("pn", G, s, ci)] = s_v.inc(last if em else None)
            emit_ptc(v, 7, 0)
            emit_avev(v, 7)

        def gpsimd_stream(gp):
            if b == 0:
                heads = list(range(0, 10))
            else:
                heads = [hg for hg in range(4 * b + 6, 4 * b + 10) if hg < 32]
            for hg in heads:
                if hg >= 2:
                    pg, ph = (hg - 2) // 4, (hg - 2) % 4
                    if em:
                        gp.wait_ge(s_t.h, LBL[("qmm", pg, ph, 1)])
                ins = None
                if em:
                    ins = gp.dma_start(out=wq_sb[hg % 2][:], in_=wqT[hg])
                s_wq[hg % 2].inc(ins, 16)
        def sync_stream(s):
            pass

        return (tensor_stream, scalar_stream, vector_stream, gpsimd_stream,
                sync_stream)

    for b in range(8):
        if b == 0:
            sb_ctr = 0
            pt_ctr = 0
        else:
            sb_ctr = LBL[("sbctr", b - 1)]
            pt_ctr = LBL[("ptctr", b - 1)]
        for s in range(8):
            E = ES[s]
            nch = 2 if E > 8 else 1
            for idx in range(4 * nch):
                LBL[("bank", b, s, idx)] = sb_ctr % 2
                sb_ctr += 1
            for r in range(E // 2):
                LBL[("ptb", b, s, r)] = pt_ctr % 2
                pt_ctr += 1
        LBL[("sbctr", b)] = sb_ctr
        LBL[("ptctr", b)] = pt_ctr

        def _snap():
            return (s_v.v, s_s.v, s_t.v, s_wq[0].v, s_wq[1].v,
                    s_wo[0].v, s_wo[1].v, s_wf3[0].v, s_wf3[1].v,
                    s_xt3[0].v, s_xt3[1].v)

        def _restore(t_):
            (s_v.v, s_s.v, s_t.v, s_wq[0].v, s_wq[1].v,
             s_wo[0].v, s_wo[1].v, s_wf3[0].v, s_wf3[1].v,
             s_xt3[0].v, s_xt3[1].v) = t_

        save = _snap()
        ts, ss, vs, gs, sy = group_pass(b, False)
        ts(None); ss(None); vs(None); gs(None); sy(None)
        endv = _snap()
        _restore(save)
        ts, ss, vs, gs, sy = group_pass(b, True)
        with nc.Block() as blk:
            blk.tensor(ts)
            blk.scalar(ss)
            blk.vector(vs)
            blk.gpsimd(gs)
            blk.sync(sy)
        assert _snap() == endv, (b,)

    es2a.close()
    es2ps.close()

    wo_sb = [sbt(f"wo_sb{i}", [128, D], BF16, es3) for i in range(2)]
    wf3 = [sbt(f"wf3_{i}", [128, 2048], F32, es3) for i in range(2)]
    qa_sb = sbt("qa_sb", [128, 2048], BF16, es3)
    qb_sb = sbt("qb_sb", [128, 2048], BF16, es3)
    wq3 = [sbt(f"wq3_{i}", [128, D], BF16, es3) for i in range(2)]
    xT_sb = [sbt(f"xT_sb{i}", [128, 512], F32, es3) for i in range(2)]
    ffr_sb = sbt("ffr_sb", [128, 512], F32, es3)
    out_sb = [sbt(f"out_sb{i}", [128, 512], F32, es3) for i in range(2)]

    fps = [pst(f"fps{i}", [128, 512], F32, es3) for i in range(2)]
    ops_ = [pst(f"ops{i}", [128, 512], F32, es3) for i in range(2)]

    def b3_pass(em, L):
        def sync_stream(s):
            for tc in range(2):
                ins = None
                if em:
                    ins = s.dma_start(
                        out=xT_sb[tc][:],
                        in_=xTd[0:128, tc * 512:(tc + 1) * 512])
                L[("xtd", 0, tc)] = s_xt3[tc].inc(ins, 16)
            for cc in range(32):
                nx = cc + 1
                if nx < 32:
                    for tc in range(2):
                        if em:
                            s.wait_ge(s_v.h, L[("ffr", cc, tc)])
                        ins = None
                        if em:
                            ins = s.dma_start(
                                out=xT_sb[tc][:],
                                in_=xTd[nx * 128:(nx + 1) * 128,
                                        tc * 512:(tc + 1) * 512])
                        L[("xtd", nx, tc)] = s_xt3[tc].inc(ins, 16)
                for tc in range(2):
                    if em:
                        s.wait_ge(s_v.h, L[("fin", cc, tc)])
                    ins = None
                    if em:
                        ins = s.dma_start(
                            out=outT[cc * 128:(cc + 1) * 128,
                                     tc * 512:(tc + 1) * 512],
                            in_=out_sb[tc][:])
                    L[("st", cc, tc)] = s_st2[tc].inc(ins, 16)
            if em:
                s.wait_ge(s_st2[0].h, L[("st", 31, 0)])
                s.wait_ge(s_st2[1].h, L[("st", 31, 1)])

        def gpsimd_stream(gp):
            for cc in range(2):
                ins = gp.dma_start(out=wo_sb[cc % 2][:], in_=woT[cc]) if em else None
                L[("wod", cc)] = s_wo[cc % 2].inc(ins, 16)
            for cc in range(32):
                nx = cc + 2
                if nx < 32:
                    if em:
                        gp.wait_ge(s_t.h, L[("to", cc, 1)])
                    ins = gp.dma_start(out=wo_sb[nx % 2][:], in_=woT[nx]) if em else None
                    L[("wod", nx)] = s_wo[nx % 2].inc(ins, 16)

        def scalar_stream3(sc):
            for half in range(2):
                ins = None
                if em:
                    ins = sc.dma_start(
                        out=wf3[half][:],
                        in_=wfTd[0][:, half * 2048:(half + 1) * 2048])
                L[("wfd", 0, half)] = s_wf3[half].inc(ins, 16)
            for cc in range(32):
                nx = cc + 1
                if nx < 32:
                    for half in range(2):
                        if em:
                            sc.wait_ge(s_v.h, L[("qsub", cc, half)])
                        ins = None
                        if em:
                            ins = sc.dma_start(
                                out=wf3[half][:],
                                in_=wfTd[nx][:, half * 2048:(half + 1) * 2048])
                        L[("wfd", nx, half)] = s_wf3[half].inc(ins, 16)

        def vector_stream(v):
            def quant(q):
                for half in range(2):
                    if em:
                        v.wait_ge(s_wf3[half].h, L[("wfd", q, half)])
                        if q >= 2:
                            v.wait_ge(s_t.h, L[("tf", q - 2, 1)])
                        if (q, half) != (0, 0):
                            pq = (q, 0) if half == 1 else (q - 1, 1)
                            v.wait_ge(s_v.h, L[("qsub", pq[0], pq[1])])
                    la = s_v.inc(v.tensor_scalar(
                        qa_sb[:], wf3[half][:], consts[:, 2:3], None,
                        op0=ALU.is_gt) if em else None)
                    lb = s_v.inc(v.tensor_scalar(
                        qb_sb[:], wf3[half][:], consts[:, 3:4], None,
                        op0=ALU.is_lt) if em else None)
                    if em:
                        v.wait_ge(s_v.h, lb)
                    L[("qsub", q, half)] = s_v.inc(v.tensor_sub(
                        wq3[q % 2][:, half * 2048:(half + 1) * 2048],
                        qa_sb[:], qb_sb[:]) if em else None)
            quant(0)
            for cc in range(32):
                if cc + 1 < 32:
                    quant(cc + 1)
                for tc in range(2):
                    if em:
                        v.wait_ge(s_t.h, L[("tf", cc, tc)])
                        v.wait_ge(s_xt3[tc].h, L[("xtd", cc, tc)])
                        if (cc, tc) != (0, 0):
                            pf = (cc, 0) if tc == 1 else (cc - 1, 1)
                            v.wait_ge(s_v.h, L[("fin", pf[0], pf[1])])
                    L[("ffr", cc, tc)] = s_v.inc(v.scalar_tensor_tensor(
                        ffr_sb[:], fps[tc][:], consts[:, 1:2], xT_sb[tc][:],
                        op0=ALU.mult, op1=ALU.add) if em else None)
                    if em:
                        v.wait_ge(s_t.h, L[("to", cc, tc)])
                        v.wait_ge(s_v.h, L[("ffr", cc, tc)])
                        if cc > 0:
                            v.wait_ge(s_st2[tc].h, L[("st", cc - 1, tc)])
                    L[("fin", cc, tc)] = s_v.inc(v.scalar_tensor_tensor(
                        out_sb[tc][:], ops_[tc][:], obias[:, cc:cc + 1], ffr_sb[:],
                        op0=ALU.add, op1=ALU.add) if em else None)

        def tensor_stream(t):
            for cc in range(32):
                if em:
                    t.wait_ge(s_wo[cc % 2].h, L[("wod", cc)])
                for tc in range(2):
                    if em:
                        t.wait_ge(s_v.h, L[("qsub", cc, 1)])
                        if cc > 0:
                            t.wait_ge(s_v.h, L[("ffr", cc - 1, tc)])
                    last = None
                    for dc in range(32):
                        if em:
                            last = t.matmul(
                                fps[tc][:], wq3[cc % 2][:, dc * 128:(dc + 1) * 128],
                                xnTo.ap()[:, dc * 1024 + tc * 512:
                                          dc * 1024 + (tc + 1) * 512],
                                start=(dc == 0), stop=(dc == 31))
                    L[("tf", cc, tc)] = s_t.inc(last if em else None)
                    if em and cc > 0:
                        t.wait_ge(s_v.h, L[("fin", cc - 1, tc)])
                    last = None
                    for h in range(32):
                        if em:
                            last = t.matmul(
                                ops_[tc][:], wo_sb[cc % 2][:, h * 128:(h + 1) * 128],
                                attnT.ap()[:, h * 1024 + tc * 512:
                                           h * 1024 + (tc + 1) * 512],
                                start=(h == 0), stop=(h == 31))
                    L[("to", cc, tc)] = s_t.inc(last if em else None)

        return (sync_stream, gpsimd_stream, vector_stream, tensor_stream,
                scalar_stream3)

    L3 = {}

    def _snap3():
        return (s_v.v, s_s.v, s_t.v, s_st2[0].v, s_st2[1].v,
                s_wo[0].v, s_wo[1].v, s_wf3[0].v, s_wf3[1].v,
                s_xt3[0].v, s_xt3[1].v)

    def _restore3(t_):
        (s_v.v, s_s.v, s_t.v, s_st2[0].v, s_st2[1].v,
         s_wo[0].v, s_wo[1].v, s_wf3[0].v, s_wf3[1].v,
         s_xt3[0].v, s_xt3[1].v) = t_

    save = _snap3()
    sy, gp_, vv, tt_, sc_ = b3_pass(False, L3)
    gp_(None); sc_(None); vv(None); tt_(None); sy(None)
    endv = _snap3()
    _restore3(save)
    sy, gp_, vv, tt_, sc_ = b3_pass(True, L3)
    with nc.Block() as blk:
        blk.sync(sy)
        blk.gpsimd(gp_)
        blk.scalar(sc_)
        blk.vector(vv)
        blk.tensor(tt_)
    assert _snap3() == endv

    return nc


def _storage_order(parity):
    if parity == 0:
        return [15 - i for i in range(16)]
    o = []
    for j in range(8):
        o += [14 - 2 * j, 15 - 2 * j]
    return o


_NC_CACHE = None


def kernel(L0_attn_x, L0_attn_wq_w, L0_attn_wq_b, L0_attn_wk_w, L0_attn_wk_b,
           L0_attn_wv_w, L0_attn_wv_b, L0_attn_wo_w, L0_attn_wo_b,
           L0_ffn_weight, L0_ffn_bias):
    import ml_dtypes
    global _NC_CACHE

    x = np.ascontiguousarray(np.asarray(L0_attn_x, dtype=np.float32))
    wq = np.asarray(L0_attn_wq_w, np.float32)
    wk = np.asarray(L0_attn_wk_w, np.float32)
    wv = np.asarray(L0_attn_wv_w, np.float32)
    wo = np.asarray(L0_attn_wo_w, np.float32)
    wf = np.asarray(L0_ffn_weight, np.float32)

    wqT = np.ascontiguousarray(wq.reshape(32, 128, 32, 128).transpose(0, 3, 2, 1)).reshape(32, 128, D)
    wkT = np.ascontiguousarray(
        wk.reshape(128, 32, 128).transpose(2, 1, 0)).reshape(128, D)
    wvT = np.ascontiguousarray(
        wv.reshape(128, 32, 128).transpose(2, 1, 0)).reshape(128, D)
    woT = np.ascontiguousarray(wo.reshape(32, 128, 32, 128).transpose(0, 3, 2, 1)).reshape(32, 128, D)
    wfT = np.ascontiguousarray(wf.reshape(32, 128, 32, 128).transpose(0, 3, 2, 1)).reshape(32, 128, D)
    wqb = np.ascontiguousarray(np.asarray(L0_attn_wq_b, np.float32).reshape(32, 128).T)
    wkb = np.ascontiguousarray(np.asarray(L0_attn_wk_b, np.float32).reshape(128, 1))
    wvb = np.ascontiguousarray(np.asarray(L0_attn_wv_b, np.float32).reshape(128, 1))
    wob = np.ascontiguousarray(np.asarray(L0_attn_wo_b, np.float32).reshape(32, 128).T)
    fbb = np.ascontiguousarray(np.asarray(L0_ffn_bias, np.float32).reshape(32, 128).T)
    identb = np.eye(128, dtype=np.float32).astype(ml_dtypes.bfloat16)
    identf = np.eye(128, dtype=np.float32)

    r = np.arange(128)
    tri = np.where(r[:, None] >= r[None, :], 0.0, -1.0e30).astype(np.float32)
    zeros = np.zeros((128, 128), np.float32)
    minf = np.full((128, 128), -1.0e30, np.float32)

    in_maps = []
    perms = []
    for c in range(8):
        b, parity = c // 2, c % 2
        sigma = _storage_order(parity)
        perm = np.concatenate([np.arange(g * 128, (g + 1) * 128) for g in sigma])
        perms.append(perm)
        xb = x[b]
        xtok_c = np.ascontiguousarray(xb[perm])
        own_rows = np.concatenate(
            [perm[2 * s * 128:(2 * s + 1) * 128] for s in range(8)])
        xT_c = np.ascontiguousarray(xb[own_rows].T)
        mask2_c = np.ascontiguousarray(
            np.concatenate([tri, zeros if parity == 0 else minf], axis=1))
        wfsl_c = np.ascontiguousarray(wf.reshape(8, -1)[c].reshape(128, 16384))
        in_maps.append({
            "xtok": xtok_c, "xT": xT_c, "wqT": wqT, "wkT": wkT, "wvT": wvT,
            "woT": woT, "wfT": wfT, "wfsl": wfsl_c, "wqb": wqb, "wkb": wkb,
            "wvb": wvb, "wob": wob, "fbb": fbb, "mask2": mask2_c,
            "identb": identb, "identf": identf,
        })

    if _NC_CACHE is None:
        _NC_CACHE = build_nc()
    nc = _NC_CACHE

    res = run_bass_kernel_spmd(nc, in_maps, core_ids=list(range(8)))

    out = np.empty((B, T, D), dtype=np.float32)
    for c in range(8):
        b = c // 2
        perm = perms[c]
        own_rows = np.concatenate(
            [perm[2 * s * 128:(2 * s + 1) * 128] for s in range(8)])
        out[b][own_rows] = res.results[c]["outT"].T
    return out


# version 15 (final)
# speedup vs baseline: 1.1664x; 1.0247x over previous
import numpy as np
from contextlib import ExitStack

import concourse.bass as bass
import concourse.mybir as mybir
from concourse.bass_utils import run_bass_kernel_spmd

F32 = mybir.dt.float32
BF16 = mybir.dt.bfloat16
AF = mybir.ActivationFunctionType
ALU = mybir.AluOpType
AX = mybir.AxisListType

B, T, D, H, HD = 4, 2048, 4096, 32, 128
NT = 16
ES = [16, 14, 12, 10, 8, 6, 4, 2]
RS = float(1.0 / np.sqrt(128.0))


class Sem:
    def __init__(self, nc, name):
        self.h = nc.alloc_semaphore(name)
        self.v = 0

    def inc(self, instr, n=1):
        if instr is not None:
            instr.then_inc(self.h, n)
        self.v += n
        return self.v


def pieces(width, maxw=512):
    out, off = [], 0
    while off < width:
        w = min(maxw, width - off)
        out.append((off, w))
        off += w
    return out


def slot_chunks(E):
    if E > 8:
        return [(0, 8 * 128), (8 * 128, (E - 8) * 128)]
    return [(0, E * 128)]


def build_nc():
    nc = bass.Bass("TRN2", target_bir_lowering=False)

    xtok = nc.dram_tensor("xtok", [T, D], F32, kind="ExternalInput").ap()
    xTd = nc.dram_tensor("xT", [D, 1024], F32, kind="ExternalInput").ap()
    wqT = nc.dram_tensor("wqT", [H, 128, D], F32, kind="ExternalInput").ap()
    wkT = nc.dram_tensor("wkT", [128, D], F32, kind="ExternalInput").ap()
    wvT = nc.dram_tensor("wvT", [128, D], F32, kind="ExternalInput").ap()
    woT = nc.dram_tensor("woT", [32, 128, D], F32, kind="ExternalInput").ap()
    wfTd = nc.dram_tensor("wfT", [32, 128, D], F32, kind="ExternalInput").ap()
    wfsl = nc.dram_tensor("wfsl", [128, 16384], F32, kind="ExternalInput").ap()
    wqb = nc.dram_tensor("wqb", [128, 32], F32, kind="ExternalInput").ap()
    wkb = nc.dram_tensor("wkb", [128, 1], F32, kind="ExternalInput").ap()
    wvb = nc.dram_tensor("wvb", [128, 1], F32, kind="ExternalInput").ap()
    wob = nc.dram_tensor("wob", [128, 32], F32, kind="ExternalInput").ap()
    fbb = nc.dram_tensor("fbb", [128, 32], F32, kind="ExternalInput").ap()
    mask2 = nc.dram_tensor("mask2", [128, 256], F32, kind="ExternalInput").ap()
    identb = nc.dram_tensor("identb", [128, 128], BF16, kind="ExternalInput").ap()
    identf = nc.dram_tensor("identf", [128, 128], F32, kind="ExternalInput").ap()
    outT = nc.dram_tensor("outT", [D, 1024], F32, kind="ExternalOutput").ap()

    cc_in = nc.dram_tensor("cc_in", [1, 2], F32, kind="Internal")
    cc_out = nc.dram_tensor("cc_out", [8, 2], F32, kind="Internal")

    es = ExitStack()
    es_kv = ExitStack()
    es0 = ExitStack()
    es2 = ExitStack()
    es2a = ExitStack()
    es2ps = ExitStack()
    es3 = ExitStack()

    def sbt(name, shape, dt, stack):
        return stack.enter_context(nc.sbuf_tensor(name, shape, dt))

    def pst(name, shape, dt, stack):
        return stack.enter_context(nc.psum_tensor(name, shape, dt))

    xnTo = sbt("xnTo", [128, 32 * 1024], BF16, es)
    m2 = sbt("m2", [128, 256], F32, es)
    idb = sbt("idb", [128, 128], BF16, es)
    idf = sbt("idf", [128, 128], F32, es)
    qb_s = sbt("qb_s", [128, 32], F32, es)
    qb2 = sbt("qb2", [128, 32], F32, es)
    kb_s = sbt("kb_s", [128, 1], F32, es)
    vb_s = sbt("vb_s", [128, 1], F32, es)
    ob_s = sbt("ob_s", [128, 32], F32, es)
    fb_s = sbt("fb_s", [128, 32], F32, es)
    obias = sbt("obias", [128, 32], F32, es)
    ones_c = sbt("ones_c", [128, 1], F32, es)
    ones8 = sbt("ones8", [8, 1], F32, es)
    ones_r = sbt("ones_r", [1, 128], F32, es)
    ones_cb = sbt("ones_cb", [128, 1], BF16, es)
    consts = sbt("consts", [128, 4], F32, es)
    absrows = sbt("absrows", [128, 16], F32, es)
    wfabs = sbt("wfabs", [128, 8], F32, es)
    cols2 = sbt("cols2", [128, 2], F32, es)
    tot_sb = sbt("tot_sb", [1, 2], F32, es)
    g_sb = sbt("g_sb", [8, 2], F32, es)
    row4 = sbt("row4", [1, 4], F32, es)

    kTb = sbt("kTb", [128, T], BF16, es_kv)
    vtok = sbt("vtok", [128, NT * 128], BF16, es_kv)

    xnTo_r = xnTo.ap().rearrange("p (dc t) -> p dc t", dc=32)

    s_ldx = [Sem(nc, "s_ldx0"), Sem(nc, "s_ldx1")]
    s_wfp = [Sem(nc, "s_wfp0"), Sem(nc, "s_wfp1")]
    s_wq = [Sem(nc, "s_wq0"), Sem(nc, "s_wq1")]
    s_wo = [Sem(nc, "s_wo0"), Sem(nc, "s_wo1")]
    s_wf3 = [Sem(nc, "s_wf30"), Sem(nc, "s_wf31")]
    s_xt3 = [Sem(nc, "s_xt30"), Sem(nc, "s_xt31")]
    s_st2 = [Sem(nc, "s_st20"), Sem(nc, "s_st21")]
    s_ldf = Sem(nc, "s_ldf")
    s_ldc = Sem(nc, "s_ldc")
    s_ldw = Sem(nc, "s_ldw")
    s_v = Sem(nc, "s_v")
    s_s = Sem(nc, "s_s")
    s_t = Sem(nc, "s_t")
    s_cc = Sem(nc, "s_cc")
    s_st = Sem(nc, "s_st")
    s_io = Sem(nc, "s_io")

    xt = [sbt(f"xt{i}", [128, D], F32, es0) for i in range(2)]
    xnt = [sbt(f"xnt{i}", [128, D], BF16, es0) for i in range(2)]
    xnf = [sbt(f"xnf{i}", [128, D], BF16, es0) for i in range(2)]
    wf0 = [sbt(f"wf0_{i}", [128, 2048], F32, es0) for i in range(2)]
    wk_sb = sbt("wk_sb", [128, D], BF16, es0)
    wv_sb = sbt("wv_sb", [128, D], BF16, es0)
    vtt = sbt("vtt", [128, 128], BF16, es0)
    bnst = sbt("bnst", [128, 48], F32, es0)
    stat = sbt("stat", [128, 8], F32, es0)
    dump = sbt("dump", [128, D], BF16, es0)
    kvK = pst("kvK", [128, 128], F32, es0)
    kvV = pst("kvV", [128, 128], F32, es0)
    tps = [pst(f"tps{i}", [128, 512], BF16, es0) for i in range(2)]
    vtp = pst("vtp", [128, 128], BF16, es0)

    with nc.Block() as blk:
        @blk.sync
        def _(s):
            for dst, src in ((m2, mask2), (idb, identb), (idf, identf),
                             (qb_s, wqb), (kb_s, wkb), (vb_s, wvb),
                             (ob_s, wob), (fb_s, fbb)):
                s_ldc.inc(s.dma_start(out=dst[:], in_=src), 16)
            for tt in range(2):
                s_ldx[tt % 2].inc(s.dma_start(
                    out=xt[tt][:],
                    in_=xtok[tt * 128:(tt + 1) * 128, :]), 16)
            for ci in range(2):
                s_wfp[ci % 2].inc(s.dma_start(
                    out=wf0[ci][:],
                    in_=wfsl[:, ci * 2048:(ci + 1) * 2048]), 16)

        @blk.gpsimd
        def _(g):
            s_ldw.inc(g.dma_start(out=wk_sb[:], in_=wkT), 16)
            s_ldw.inc(g.dma_start(out=wv_sb[:], in_=wvT), 16)

        @blk.vector
        def _(v):
            v.memset(ones_c[:], 1.0)
            v.memset(ones_cb[:], 1.0)
            v.memset(ones8[:], 1.0)
            v.memset(ones_r[:], 1.0)

    ldc_all = s_ldc.v
    ldw_kv = s_ldw.v

    LP = {}

    def nwf(tt):
        return 1 if tt < 8 else 0

    def p0_pass(em):
        def sync_stream(s):
            for tt in range(NT):
                if tt + 2 < NT:
                    if em:
                        s.wait_ge(s_s.h, LP[("xn", tt)])
                    ins = None
                    if em:
                        ins = s.dma_start(
                            out=xt[tt % 2][:],
                            in_=xtok[(tt + 2) * 128:(tt + 3) * 128, :])
                    s_ldx[tt % 2].inc(ins, 16)
                ci = tt + 2
                if 2 <= ci < 8:
                    if em:
                        s.wait_ge(s_s.h, LP[("wfa", tt)])
                    ins = None
                    if em:
                        ins = s.dma_start(
                            out=wf0[ci % 2][:],
                            in_=wfsl[:, ci * 2048:(ci + 1) * 2048])
                    s_wfp[ci % 2].inc(ins, 16)

        def vector_stream(v):
            for it in range(NT + 2):
                if it < NT:
                    tt = it
                    p = tt % 2
                    if em:
                        v.wait_ge(s_ldx[p].h, 16 * (tt // 2 + 1))
                        if tt == 0:
                            v.wait_ge(s_ldc.h, ldc_all)
                        if tt >= 1:
                            v.wait_ge(s_v.h, LP[("aggr", tt - 1)])
                        if tt >= 1:
                            v.wait_ge(s_s.h, LP[("ln", tt - 1)])
                    last = None
                    for i in range(8):
                        if em:
                            last = v.bn_stats(bnst[:, i * 6:(i + 1) * 6],
                                              xt[p][:, i * 512:(i + 1) * 512])
                    lb = s_v.inc(last if em else None)
                    if em:
                        v.wait_ge(s_v.h, lb)
                    LP[("aggr", tt)] = s_v.inc(
                        v.bn_aggr(stat[:, 0:2], bnst[:]) if em else None)
                    if em:
                        v.wait_ge(s_v.h, LP[("aggr", tt)])
                    LP[("veps", tt)] = s_v.inc(
                        v.tensor_scalar(stat[:, 5:6], stat[:, 1:2], 1.0e-5,
                                        None, op0=ALU.add) if em else None)
                    if em:
                        v.wait_ge(s_s.h, LP[("rstd", tt)])
                        if tt >= 1:
                            v.wait_ge(s_s.h, LP[("xn", tt - 1)])
                    LP[("nmr", tt)] = s_v.inc(
                        v.scalar_tensor_tensor(
                            stat[:, 3:4], stat[:, 0:1], -1.0, stat[:, 2:3],
                            op0=ALU.mult, op1=ALU.mult) if em else None)
                if 1 <= it <= NT:
                    tt = it - 1
                    own = (tt % 2 == 0)
                    slot = tt // 2
                    p = tt % 2
                    for qd in range(8):
                        if em:
                            v.wait_ge(s_t.h, LP[("tp", tt, qd)])
                        dc0 = qd * 4
                        src_ = tps[qd % 2].ap().rearrange("p (a b) -> p a b", a=4)
                        ins = None
                        if own:
                            if em:
                                ins = v.tensor_copy(
                                    xnTo_r[:, dc0:dc0 + 4,
                                           slot * 128:(slot + 1) * 128], src_)
                        else:
                            if em:
                                if qd == 0 and tt >= 2:
                                    v.wait_ge(s_t.h, LP[("kvV", tt - 2)])
                                ins = v.tensor_copy(
                                    xnf[p].ap().rearrange(
                                        "p (a b) -> p a b",
                                        a=32)[:, dc0:dc0 + 4, :], src_)
                        LP[("ev", tt, qd)] = s_v.inc(ins)
                if 2 <= it:
                    tt = it - 2
                    if em:
                        v.wait_ge(s_t.h, LP[("vtp", tt)])
                    LP[("vte", tt)] = s_v.inc(
                        v.tensor_copy(vtok[:, tt * 128:(tt + 1) * 128], vtp[:])
                        if em else None)

        def scalar_stream(s_):
            for it in range(NT + 2):
                if it < NT:
                    tt = it
                    p = tt % 2
                    if em:
                        s_.wait_ge(s_v.h, LP[("veps", tt)])
                    LP[("ln", tt)] = s_s.inc(
                        s_.activation(stat[:, 4:5], stat[:, 5:6], AF.Ln,
                                      bias=0.0, scale=1.0) if em else None)
                    if em:
                        s_.wait_ge(s_s.h, LP[("ln", tt)])
                    LP[("rstd", tt)] = s_s.inc(
                        s_.activation(stat[:, 2:3], stat[:, 4:5], AF.Exp,
                                      bias=0.0, scale=-0.5) if em else None)
                    if em:
                        s_.wait_ge(s_v.h, LP[("nmr", tt)])
                        s_.wait_ge(s_s.h, LP[("rstd", tt)])
                        if tt >= 2:
                            s_.wait_ge(s_t.h, LP[("tp", tt - 2, 7)])
                    LP[("xn", tt)] = s_s.inc(
                        s_.activation(xnt[p][:], xt[p][:], AF.Identity,
                                      bias=stat[:, 3:4], scale=stat[:, 2:3])
                        if em else None)
                if 1 <= it <= NT:
                    tt = it - 1
                    p = tt % 2
                    if em:
                        s_.wait_ge(s_s.h, LP[("xn", tt)])
                        if tt >= 1:
                            s_.wait_ge(s_s.h, LP[("dmp", tt - 1)])
                    LP[("abs", tt)] = s_s.inc(
                        s_.activation(dump[:], xnt[p][:], AF.Abs,
                                      accum_out=absrows[:, tt:tt + 1])
                        if em else None)
                    LP[("dmp", tt)] = LP[("abs", tt)]
                    if nwf(tt):
                        if em:
                            s_.wait_ge(s_wfp[p].h, 16 * (tt // 2 + 1))
                            s_.wait_ge(s_s.h, LP[("abs", tt)])
                        LP[("wfa", tt)] = s_s.inc(
                            s_.activation(dump[:, 0:2048], wf0[p][:], AF.Abs,
                                          accum_out=wfabs[:, tt:tt + 1])
                            if em else None)
                        LP[("dmp", tt)] = LP[("wfa", tt)]
                    else:
                        LP[("wfa", tt)] = LP[("abs", tt)]
                if 2 <= it:
                    tt = it - 2
                    if em:
                        s_.wait_ge(s_t.h, LP[("kvK", tt)])
                    LP[("ke", tt)] = s_s.inc(
                        s_.activation(kTb[:, tt * 128:(tt + 1) * 128], kvK[:],
                                      AF.Identity, bias=kb_s[:]) if em else None)
                    if em:
                        s_.wait_ge(s_t.h, LP[("kvV", tt)])
                        if tt >= 1:
                            s_.wait_ge(s_t.h, LP[("vtp", tt - 1)])
                        s_.wait_ge(s_s.h, LP[("ke", tt)])
                    LP[("ve", tt)] = s_s.inc(
                        s_.activation(vtt[:], kvV[:], AF.Identity, bias=vb_s[:])
                        if em else None)

        def tensor_stream(t):
            for it in range(NT + 1):
                if it < NT:
                    tt = it
                    p = tt % 2
                    if em:
                        t.wait_ge(s_s.h, LP[("xn", tt)])
                        if tt == 0:
                            t.wait_ge(s_ldc.h, ldc_all)
                    for qd in range(8):
                        if em:
                            gq = 8 * tt + qd
                            if gq >= 2:
                                pt_, pq = (gq - 2) // 8, (gq - 2) % 8
                                t.wait_ge(s_v.h, LP[("ev", pt_, pq)])
                        last = None
                        for i in range(4):
                            dc = qd * 4 + i
                            if em:
                                last = t.transpose(
                                    tps[qd % 2][:, i * 128:(i + 1) * 128],
                                    xnt[p][:, dc * 128:(dc + 1) * 128], idb[:])
                        LP[("tp", tt, qd)] = s_t.inc(last if em else None)
                if it >= 1:
                    tt = it - 1
                    own = (tt % 2 == 0)
                    slot = tt // 2
                    p = tt % 2
                    if em:
                        t.wait_ge(s_v.h, LP[("ev", tt, 7)])
                        if tt == 0:
                            t.wait_ge(s_ldw.h, ldw_kv)
                        if tt >= 1:
                            t.wait_ge(s_s.h, LP[("ve", tt - 1)])
                    for wi, ps in ((0, kvK), (1, kvV)):
                        w = wk_sb if wi == 0 else wv_sb
                        last = None
                        for dc in range(32):
                            if em:
                                if own:
                                    rhs = xnTo.ap()[:, dc * 1024 + slot * 128:
                                                    dc * 1024 + (slot + 1) * 128]
                                else:
                                    rhs = xnf[p].ap()[:, dc * 128:(dc + 1) * 128]
                                last = t.matmul(ps[:],
                                                w[:, dc * 128:(dc + 1) * 128],
                                                rhs, start=(dc == 0),
                                                stop=(dc == 31))
                        LP[("kvK" if wi == 0 else "kvV", tt)] = s_t.inc(
                            last if em else None)
                    if em:
                        t.wait_ge(s_s.h, LP[("ve", tt)])
                        if tt >= 1:
                            t.wait_ge(s_v.h, LP[("vte", tt - 1)])
                    LP[("vtp", tt)] = s_t.inc(
                        t.transpose(vtp[:], vtt[:], idb[:]) if em else None)

        return sync_stream, vector_stream, scalar_stream, tensor_stream

    def _snap0():
        return (s_v.v, s_s.v, s_t.v, s_ldx[0].v, s_ldx[1].v,
                s_wfp[0].v, s_wfp[1].v)

    def _restore0(t_):
        (s_v.v, s_s.v, s_t.v, s_ldx[0].v, s_ldx[1].v,
         s_wfp[0].v, s_wfp[1].v) = t_

    save0 = _snap0()
    sy0, vv0, ss0, tt0 = p0_pass(False)
    sy0(None); vv0(None); ss0(None); tt0(None)
    endv0 = _snap0()
    _restore0(save0)
    sy0, vv0, ss0, tt0 = p0_pass(True)
    with nc.Block() as blk:
        blk.sync(sy0)
        blk.vector(vv0)
        blk.scalar(ss0)
        blk.tensor(tt0)
    assert _snap0() == endv0


    es0.close()

    attnT = sbt("attnT", [128, 32 * 1024], BF16, es2)
    qTt = [sbt(f"qTt{i}", [128, 4 * 1024], BF16, es2a) for i in range(2)]
    wq_sb = [sbt(f"wq_sb{i}", [128, D], BF16, es2a) for i in range(2)]
    Pt = [sbt("Pt0", [128, 4 * 2048], BF16, es2a),
          sbt("Pt1", [128, 4 * 1792], BF16, es2a)]
    PS = [2048, 1792]
    sqb = Pt[1].ap()[:, 0:512]
    da_b = sbt("da_b", [128, 8], F32, es2a)
    db_b = sbt("db_b", [128, 8], F32, es2a)
    dsum = sbt("dsum", [128, 4], F32, es2a)
    rd8 = sbt("rd8", [128, 8], F32, es2a)
    qmx = sbt("qmx", [1, 8], F32, es2a)
    qm4 = sbt("qm4", [1, 4], F32, es2a)
    kmx = sbt("kmx", [1, 8], F32, es2a)
    c4r = sbt("c4r", [1, 4], F32, es2a)
    negc = sbt("negc", [128, 4], F32, es2a)
    msk_sb = [sbt(f"msk_sb{i}", [128, 256], F32, es2a) for i in range(2)]
    ptm_sb = [sbt(f"ptm_sb{i}", [128, 1024], BF16, es2a) for i in range(2)]


    sAB = [pst(f"sps{i}", [128, 1024], F32, es2ps) for i in range(2)]
    ptAB = [pst(f"ptps{i}", [128, 1024], BF16, es2ps) for i in range(2)]
    avq = pst("avq", [128, 512], F32, es2ps)
    qps = pst("qps", [128, 512], F32, es2ps)
    attnT_r = attnT.ap().rearrange("p (h t) -> p h t", h=32)

    gstate = {
        "sbank_free": [(s_s, 0), (s_s, 0)],
        "ptbank_free": [0, 0],
        "msk_free": [0, 0],
        "av_free": 0,
        "qps_free": (s_s, 0),
    }
    LBL = {}

    def q_unit(t, em, g, u):
        h, tc = u // 2, u % 2
        hg = 4 * g + h
        if em:
            t.wait_ge(s_wq[hg % 2].h, 16 * (hg // 2 + 1))
            qsem, qval = gstate["qps_free"]
            t.wait_ge(qsem.h, qval)
        last = None
        for dc in range(32):
            if em:
                last = t.matmul(
                    qps[:], wq_sb[hg % 2][:, dc * 128:(dc + 1) * 128],
                    xnTo.ap()[:, dc * 1024 + tc * 512:dc * 1024 + (tc + 1) * 512],
                    start=(dc == 0), stop=(dc == 31))
        LBL[("qmm", g, h, tc)] = s_t.inc(last if em else None)
        if em:
            gstate["qps_free"] = (s_s, LBL[("qev", g, h, tc)])

    def qev_unit(s_e, em, g, u):
        h, tc = u // 2, u % 2
        hg = 4 * g + h
        if em:
            s_e.wait_ge(s_t.h, LBL[("qmm", g, h, tc)])
        ins = None
        if em:
            ins = s_e.activation(
                qTt[g % 2][:, h * 1024 + tc * 512:h * 1024 + (tc + 1) * 512],
                qps[:], AF.Identity, bias=qb2[:, hg:hg + 1], scale=RS)
        LBL[("qev", g, h, tc)] = s_s.inc(ins)

    def group_pass(b, em):
        G = b

        def norms_prologue_T(t):
            if b == 0 and em:
                t.wait_ge(s_s.h, LBL[("qev", 0, 3, 1)])
            if b == 0:
                for j in range(4):
                    if em:
                        t.wait_ge(s_v.h, LBL[("ksq", j)])
                        if j > 0:
                            t.wait_ge(s_v.h, LBL[("krm", j - 1)])
                    ins = None
                    if em:
                        ins = t.matmul(qps[0:1, 0:512], ones_cb[:], sqb,
                                       start=True, stop=True)
                    LBL[("knm", j)] = s_t.inc(ins)
            for p in range(8):
                if em:
                    t.wait_ge(s_v.h, LBL[("qsq", G, p)])
                    if p > 0:
                        t.wait_ge(s_v.h, LBL[("qrm", G, p - 1)])
                    elif b == 0:
                        t.wait_ge(s_v.h, LBL[("krm", 3)])
                ins = None
                if em:
                    ins = t.matmul(qps[0:1, 0:512], ones_cb[:], sqb,
                                   start=True, stop=True)
                LBL[("qnm", G, p)] = s_t.inc(ins)
            if em:
                t.wait_ge(s_v.h, LBL[("ncrow", G)])
            ins = None
            if em:
                ins = t.matmul(avq[:, 0:4], ones_r[:], c4r[0:1, :],
                               start=True, stop=True)
            LBL[("ncrep", G)] = s_t.inc(ins)
            if b == 1:
                if em:
                    t.wait_ge(s_io.h, LBL[("iog",)])
                ins = None
                if em:
                    ins = t.matmul(avq[0:1, 4:6], ones8[:], g_sb[:],
                                   start=True, stop=True)
                LBL[("tot2",)] = s_t.inc(ins)
                if em:
                    t.wait_ge(s_v.h, LBL[("row4",)])
                ins = None
                if em:
                    ins = t.matmul(avq[:, 8:12], ones_r[:], row4[0:1, :],
                                   start=True, stop=True)
                LBL[("consrep",)] = s_t.inc(ins)

        def tensor_stream(t):
            if b == 0:
                if em:
                    t.wait_ge(s_v.h, LBL[("cols2",)])
                ins = None
                if em:
                    ins = t.matmul(avq[0:1, 4:6], ones_c[:], cols2[:],
                                   start=True, stop=True)
                LBL[("tot",)] = s_t.inc(ins)
                for u in range(8):
                    q_unit(t, em, 0, u)
            norms_prologue_T(t)
            for s in range(9):
                if s == 8:
                    emit_ptav_rounds(t, 7, 0, ES[7] // 2)
                    break
                if b < 7:
                    if s == 0 and em:
                        t.wait_ge(s_v.h, LBL[("qrm", G, 7)])
                    q_unit(t, em, b + 1, s)
                E = ES[s]
                ch = slot_chunks(E)
                nch = len(ch)
                steps = [(h, ci) for h in range(4) for ci in range(nch)]
                nr_prev = ES[s - 1] // 2 if s >= 1 else 0
                rdone = 0
                for idx, (h, ci) in enumerate(steps):
                    bb = LBL[("bank", G, s, idx)]
                    if em:
                        sem_, val_ = gstate["sbank_free"][bb]
                        t.wait_ge(sem_.h, val_)
                        t.wait_ge(s_s.h, LBL[("qev", G, h, 1)])
                    off0, cw = ch[ci]
                    last = None
                    for (off, w) in pieces(cw):
                        if em:
                            last = t.matmul(
                                sAB[bb][:, off:off + w],
                                qTt[G % 2][:, h * 1024 + s * 128:
                                           h * 1024 + (s + 1) * 128],
                                kTb[:, (16 - E) * 128 + off0 + off:
                                    (16 - E) * 128 + off0 + off + w],
                                start=True, stop=True)
                    LBL[("sc", G, s, idx)] = s_t.inc(last if em else None)
                    if em:
                        gstate["sbank_free"][bb] = (s_s, LBL[("exp", G, s, idx)])
                    if s >= 1 and idx >= 1:
                        take = min(1 + rdone, nr_prev) - rdone
                        if take > 0:
                            emit_ptav_rounds(t, s - 1, rdone, rdone + take)
                            rdone += take
                if s >= 1 and rdone < nr_prev:
                    emit_ptav_rounds(t, s - 1, rdone, nr_prev)

        def emit_ptav_rounds(t, s, r0, r1):
            E = ES[s]
            nr = E // 2
            for r in range(r0, r1):
                pb = LBL[("ptb", G, s, r)]
                ps_ = PS[s % 2]
                csem = s_v if pb == 0 else s_s
                if em:
                    t.wait_ge(csem.h, gstate["ptbank_free"][pb])
                    ci_hi = 0 if (2 * r + 1) < 8 else 1
                    t.wait_ge(s_v.h, LBL[("pn", G, s, ci_hi)])
                last = None
                for j in range(2):
                    kk = 2 * r + j
                    for h in range(4):
                        if em:
                            last = t.transpose(
                                ptAB[pb][:, j * 512 + h * 128:
                                         j * 512 + (h + 1) * 128],
                                Pt[s % 2][:, h * ps_ + kk * 128:
                                          h * ps_ + (kk + 1) * 128],
                                idb[:])
                LBL[("pt", G, s, r)] = s_t.inc(last if em else None)
                if em:
                    gstate["ptbank_free"][pb] = LBL[("ptc", G, s, r)]
                if r >= 1:
                    emit_av_round(t, s, r - 1)
            if r1 == nr:
                emit_av_round(t, s, nr - 1)
                if em:
                    gstate["av_free"] = LBL[("avev", G, s)]

        def emit_av_round(t, s, r):
            E = ES[s]
            nr = E // 2
            pb = LBL[("ptb", G, s, r)]
            csem = s_v if pb == 0 else s_s
            if em:
                t.wait_ge(csem.h, LBL[("ptc", G, s, r)])
                if r == 0:
                    t.wait_ge(s_v.h, gstate["av_free"])
            for j in range(2):
                kk = 2 * r + j
                kabs = 16 - E + kk
                ins = None
                if em:
                    ins = t.matmul(avq[:],
                                   vtok[:, kabs * 128:(kabs + 1) * 128],
                                   ptm_sb[pb][:, j * 512:(j + 1) * 512],
                                   start=(kk == 0), stop=(kk == E - 1))
                LBL[("av", G, s)] = s_t.inc(ins)

        def emit_ptc(eng, s, want_pb, r0=0, r1=None):
            E = ES[s]
            nr = E // 2
            if r1 is None:
                r1 = nr
            for r in range(r0, min(r1, nr)):
                pb = LBL[("ptb", G, s, r)]
                if pb != want_pb:
                    continue
                if em:
                    eng.wait_ge(s_t.h, LBL[("pt", G, s, r)])
                ins = None
                if em:
                    if want_pb == 1:
                        ins = eng.copy(ptm_sb[pb][:], ptAB[pb][:])
                    else:
                        ins = eng.tensor_copy(ptm_sb[pb][:], ptAB[pb][:])
                sem = s_s if want_pb == 1 else s_v
                LBL[("ptc", G, s, r)] = sem.inc(ins)

        def scalar_stream(s_e):
            if b == 0:
                if em:
                    s_e.wait_ge(s_v.h, LBL[("qb2",)])
                for u in range(8):
                    qev_unit(s_e, em, 0, u)
            if em:
                s_e.wait_ge(s_v.h, LBL[("qm4", G)])
            ins = s_e.activation(c4r[0:1, :], qm4[0:1, :], AF.Ln) if em else None
            l_ln = s_s.inc(ins)
            if em:
                s_e.wait_ge(s_s.h, l_ln)
            ins = None
            if em:
                ins = s_e.activation(c4r[0:1, :], c4r[0:1, :], AF.Exp, scale=0.5)
            LBL[("c4", G)] = s_s.inc(ins)
            for s in range(8):
                E = ES[s]
                ch = slot_chunks(E)
                nch = len(ch)
                steps = [(h, ci) for h in range(4) for ci in range(nch)]
                sr_done = 0
                for idx, (h, ci) in enumerate(steps):
                    bu = LBL[("bank", G, s, idx)]
                    ps_ = PS[s % 2]
                    if em:
                        if ci == 0:
                            s_e.wait_ge(s_v.h, LBL[("msk", G, s, idx)])
                        else:
                            s_e.wait_ge(s_t.h, LBL[("sc", G, s, idx)])
                        if s == 0 and idx < 2:
                            s_e.wait_ge(s_v.h, LBL[("ncev", G)])
                    off0, cw = ch[ci]
                    col = ci * 4 + h
                    last = None
                    if ci == 0:
                        if em:
                            last = s_e.activation(
                                Pt[s % 2][:, h * ps_:h * ps_ + 256],
                                msk_sb[h % 2][:], AF.Exp,
                                bias=negc[:, h:h + 1], scale=1.0,
                                accum_out=da_b[:, col:col + 1])
                        if cw > 256:
                            if em:
                                last = s_e.activation(
                                    Pt[s % 2][:, h * ps_ + 256:h * ps_ + cw],
                                    sAB[bu][:, 256:cw], AF.Exp,
                                    bias=negc[:, h:h + 1], scale=1.0,
                                    accum_out=db_b[:, col:col + 1])
                    else:
                        if em:
                            last = s_e.activation(
                                Pt[s % 2][:, h * ps_ + 1024:h * ps_ + 1024 + cw],
                                sAB[bu][:, 0:cw], AF.Exp,
                                bias=negc[:, h:h + 1], scale=1.0,
                                accum_out=da_b[:, col:col + 1])
                    LBL[("exp", G, s, idx)] = s_s.inc(last if em else None)
                    if s >= 1:
                        emit_ptc(s_e, s - 1, 1, sr_done, idx)
                        sr_done = max(sr_done, min(idx, ES[s - 1] // 2))
                if s >= 1:
                    emit_ptc(s_e, s - 1, 1, sr_done)
                if b < 7:
                    qev_unit(s_e, em, b + 1, s)
            emit_ptc(s_e, 7, 1)

        def emit_avev(v, s):
            if em:
                v.wait_ge(s_t.h, LBL[("av", G, s)])
            ins = None
            if em:
                ins = v.tensor_copy(
                    attnT_r[:, 4 * G:4 * G + 4, s * 128:(s + 1) * 128],
                    avq.ap().rearrange("p (a b) -> p a b", a=4))
            LBL[("avev", G, s)] = s_v.inc(ins)

        def vector_stream(v):
            if b == 0:
                if em:
                    v.wait_ge(s_ldc.h, ldc_all)
                l1 = s_v.inc(v.reduce_sum(cols2[:, 0:1], absrows[:], axis=AX.X)
                             if em else None)
                LBL[("cols2",)] = s_v.inc(
                    v.reduce_sum(cols2[:, 1:2], wfabs[:], axis=AX.X)
                    if em else None)
                LBL[("qb2",)] = s_v.inc(v.tensor_scalar(
                    qb2[:], qb_s[:], RS, None, op0=ALU.mult) if em else None)
                if em:
                    v.wait_ge(s_t.h, LBL[("tot",)])
                LBL[("totc",)] = s_v.inc(
                    v.tensor_copy(tot_sb[:], avq[0:1, 4:6]) if em else None)
            if b == 0:
                for j in range(4):
                    if em:
                        if j > 0:
                            v.wait_ge(s_t.h, LBL[("knm", j - 1)])
                    ins = None
                    if em:
                        ins = v.tensor_mul(sqb, kTb[:, j * 512:(j + 1) * 512],
                                           kTb[:, j * 512:(j + 1) * 512])
                    LBL[("ksq", j)] = s_v.inc(ins)
                    if em:
                        v.wait_ge(s_t.h, LBL[("knm", j)])
                    ins = None
                    if em:
                        ins = v.reduce_max(kmx[0:1, j:j + 1], qps[0:1, 0:512],
                                           axis=AX.X)
                    LBL[("krm", j)] = s_v.inc(ins)
                if em:
                    v.wait_ge(s_v.h, LBL[("krm", 3)])
                ins = None
                if em:
                    ins = v.reduce_max(kmx[0:1, 4:5], kmx[0:1, 0:4], axis=AX.X)
                LBL[("kmax", 0)] = s_v.inc(ins)
            for p in range(8):
                h, j = p % 4, p // 4
                if em:
                    if b == 0:
                        v.wait_ge(s_s.h, LBL[("qev", G, h, 1)])
                    if p == 0 and b == 0:
                        v.wait_ge(s_t.h, LBL[("knm", 3)])
                    if p > 0:
                        v.wait_ge(s_t.h, LBL[("qnm", G, p - 1)])
                ins = None
                if em:
                    ins = v.tensor_mul(
                        sqb,
                        qTt[G % 2][:, h * 1024 + j * 512:h * 1024 + (j + 1) * 512],
                        qTt[G % 2][:, h * 1024 + j * 512:h * 1024 + (j + 1) * 512])
                LBL[("qsq", G, p)] = s_v.inc(ins)
                if em:
                    v.wait_ge(s_t.h, LBL[("qnm", G, p)])
                ins = None
                if em:
                    ins = v.reduce_max(qmx[0:1, p:p + 1], qps[0:1, 0:512],
                                       axis=AX.X)
                LBL[("qrm", G, p)] = s_v.inc(ins)
            if em:
                v.wait_ge(s_v.h, LBL[("qrm", G, 7)])
            ins = None
            if em:
                ins = v.tensor_tensor(qm4[0:1, :], qmx[0:1, 0:4],
                                      qmx[0:1, 4:8], op=ALU.max)
            l_qm = s_v.inc(ins)
            if em:
                v.wait_ge(s_v.h, l_qm)
                v.wait_ge(s_v.h, LBL[("kmax", 0)])
            ins = None
            if em:
                ins = v.tensor_scalar(qm4[0:1, :], qm4[0:1, :], kmx[0:1, 4:5],
                                      None, op0=ALU.mult)
            LBL[("qm4", G)] = s_v.inc(ins)
            if em:
                v.wait_ge(s_s.h, LBL[("c4", G)])
            ins = None
            if em:
                ins = v.tensor_scalar(c4r[0:1, :], c4r[0:1, :],
                                      -1.05 * RS, None, op0=ALU.mult)
            LBL[("ncrow", G)] = s_v.inc(ins)
            if em:
                v.wait_ge(s_t.h, LBL[("ncrep", G)])
            ins = v.tensor_copy(negc[:], avq[:, 0:4]) if em else None
            LBL[("ncev", G)] = s_v.inc(ins)
            if em and gstate["av_free"] < LBL[("ncev", G)]:
                gstate["av_free"] = LBL[("ncev", G)]
            if b == 1:
                if em:
                    v.wait_ge(s_t.h, LBL[("tot2",)])
                r0 = s_v.inc(v.tensor_scalar(
                    row4[0:1, 0:1], avq[0:1, 4:5], 1.0 / (2.0 * B * T * D),
                    1.0e-8, op0=ALU.mult, op1=ALU.max) if em else None)
                r1 = s_v.inc(v.tensor_scalar(
                    row4[0:1, 1:2], avq[0:1, 5:6], 1.0 / (float(D) * D),
                    1.0e-8, op0=ALU.mult, op1=ALU.max) if em else None)
                if em:
                    v.wait_ge(s_v.h, r1)
                r2 = s_v.inc(v.tensor_scalar(
                    row4[0:1, 2:3], row4[0:1, 1:2], 0.5, None,
                    op0=ALU.mult) if em else None)
                if em:
                    v.wait_ge(s_v.h, r2)
                LBL[("row4",)] = s_v.inc(v.tensor_scalar(
                    row4[0:1, 3:4], row4[0:1, 2:3], -1.0, None,
                    op0=ALU.mult) if em else None)
                if em:
                    v.wait_ge(s_t.h, LBL[("consrep",)])
                lc = s_v.inc(v.tensor_copy(consts[:], avq[:, 8:12])
                             if em else None)
                if em:
                    v.wait_ge(s_v.h, lc)
                s_v.inc(v.scalar_tensor_tensor(
                    obias[:], fb_s[:], consts[:, 0:1], ob_s[:],
                    op0=ALU.mult, op1=ALU.add) if em else None)
                if em and gstate["av_free"] < lc:
                    gstate["av_free"] = lc
            for s in range(8):
                E = ES[s]
                ch = slot_chunks(E)
                nch = len(ch)
                steps = [(h, ci) for h in range(4) for ci in range(nch)]
                second0 = ch[0][1] > 256
                if second0:
                    s_v.inc(v.memset(db_b[:], 0.0) if em else None)
                vr_done = 0
                for idx, (h, ci) in enumerate(steps):
                    bb = LBL[("bank", G, s, idx)]
                    if ci != 0:
                        continue
                    if em:
                        v.wait_ge(s_t.h, LBL[("sc", G, s, idx)])
                        v.wait_ge(s_s.h, gstate["msk_free"][h % 2])
                    ins = None
                    if em:
                        ins = v.tensor_add(msk_sb[h % 2][:],
                                           sAB[bb][:, 0:256], m2[:])
                    LBL[("msk", G, s, idx)] = s_v.inc(ins)
                    if em:
                        gstate["msk_free"][h % 2] = LBL[("exp", G, s, idx)]
                    if s >= 1:
                        emit_ptc(v, s - 1, 0, vr_done, idx + 1)
                        vr_done = max(vr_done, min(idx + 1, ES[s - 1] // 2))
                if s >= 1:
                    emit_ptc(v, s - 1, 0, vr_done)
                    emit_avev(v, s - 1)
                if em:
                    v.wait_ge(s_s.h, LBL[("exp", G, s, len(steps) - 1)])
                if nch == 2:
                    l1 = s_v.inc(v.tensor_add(dsum[:], da_b[:, 0:4], db_b[:, 0:4])
                                 if em else None)
                    if em:
                        v.wait_ge(s_v.h, l1)
                    l2 = s_v.inc(v.tensor_add(dsum[:], dsum[:], da_b[:, 4:8])
                                 if em else None)
                    if em:
                        v.wait_ge(s_v.h, l2)
                    LBL[("rd8", G, s)] = s_v.inc(
                        v.reciprocal(rd8[:, 0:4], dsum[:]) if em else None)
                elif second0:
                    l1 = s_v.inc(v.tensor_add(dsum[:], da_b[:, 0:4],
                                              db_b[:, 0:4]) if em else None)
                    if em:
                        v.wait_ge(s_v.h, l1)
                    LBL[("rd8", G, s)] = s_v.inc(
                        v.reciprocal(rd8[:, 0:4], dsum[:]) if em else None)
                else:
                    LBL[("rd8", G, s)] = s_v.inc(
                        v.reciprocal(rd8[:, 0:4], da_b[:, 0:4]) if em else None)
                if em:
                    v.wait_ge(s_v.h, LBL[("rd8", G, s)])
                for ci in range(nch):
                    cw = ch[ci][1]
                    ps_ = PS[s % 2]
                    last = None
                    for h in range(4):
                        if em:
                            last = v.tensor_scalar(
                                Pt[s % 2][:, h * ps_ + ci * 1024:
                                          h * ps_ + ci * 1024 + cw],
                                Pt[s % 2][:, h * ps_ + ci * 1024:
                                          h * ps_ + ci * 1024 + cw],
                                rd8[:, h:h + 1], None,
                                op0=ALU.mult)
                    LBL[("pn", G, s, ci)] = s_v.inc(last if em else None)
            emit_ptc(v, 7, 0)
            emit_avev(v, 7)

        def gpsimd_stream(gp):
            if b == 0:
                heads = list(range(0, 10))
            else:
                heads = [hg for hg in range(4 * b + 6, 4 * b + 10) if hg < 32]

            def load_head(hg):
                if hg >= 2:
                    pg, ph = (hg - 2) // 4, (hg - 2) % 4
                    if em:
                        gp.wait_ge(s_t.h, LBL[("qmm", pg, ph, 1)])
                ins = None
                if em:
                    ins = gp.dma_start(out=wq_sb[hg % 2][:], in_=wqT[hg])
                s_wq[hg % 2].inc(ins, 16)

            if b == 0:
                for hg in heads[0:4]:
                    load_head(hg)
                if em:
                    gp.wait_ge(s_v.h, LBL[("totc",)])
                ins = gp.dma_start(out=cc_in.ap(), in_=tot_sb[:]) if em else None
                s_io.inc(ins, 16)
                if em:
                    gp.wait_ge(s_io.h, s_io.v)
                ins = None
                if em:
                    ins = gp.collective_compute(
                        "AllGather", ALU.bypass,
                        replica_groups=[list(range(8))],
                        ins=[cc_in.ap().opt()], outs=[cc_out.ap().opt()])
                s_cc.inc(ins)
                for hg in heads[4:8]:
                    load_head(hg)
                if em:
                    gp.wait_ge(s_cc.h, s_cc.v)
                ins = gp.dma_start(out=g_sb[:], in_=cc_out.ap()) if em else None
                LBL[("iog",)] = s_io.inc(ins, 16)
                for hg in heads[8:]:
                    load_head(hg)
            else:
                for hg in heads:
                    load_head(hg)
        def sync_stream(s):
            pass

        return (tensor_stream, scalar_stream, vector_stream, gpsimd_stream,
                sync_stream)

    for b in range(8):
        if b == 0:
            sb_ctr = 0
            pt_ctr = 0
        else:
            sb_ctr = LBL[("sbctr", b - 1)]
            pt_ctr = LBL[("ptctr", b - 1)]
        for s in range(8):
            E = ES[s]
            nch = 2 if E > 8 else 1
            for idx in range(4 * nch):
                LBL[("bank", b, s, idx)] = sb_ctr % 2
                sb_ctr += 1
            for r in range(E // 2):
                LBL[("ptb", b, s, r)] = pt_ctr % 2
                pt_ctr += 1
        LBL[("sbctr", b)] = sb_ctr
        LBL[("ptctr", b)] = pt_ctr

        def _snap():
            return (s_v.v, s_s.v, s_t.v, s_wq[0].v, s_wq[1].v,
                    s_wo[0].v, s_wo[1].v, s_wf3[0].v, s_wf3[1].v,
                    s_xt3[0].v, s_xt3[1].v, s_io.v, s_cc.v)

        def _restore(t_):
            (s_v.v, s_s.v, s_t.v, s_wq[0].v, s_wq[1].v,
             s_wo[0].v, s_wo[1].v, s_wf3[0].v, s_wf3[1].v,
             s_xt3[0].v, s_xt3[1].v, s_io.v, s_cc.v) = t_

        save = _snap()
        ts, ss, vs, gs, sy = group_pass(b, False)
        ts(None); ss(None); vs(None); gs(None); sy(None)
        endv = _snap()
        _restore(save)
        ts, ss, vs, gs, sy = group_pass(b, True)
        with nc.Block() as blk:
            blk.tensor(ts)
            blk.scalar(ss)
            blk.vector(vs)
            blk.gpsimd(gs)
            blk.sync(sy)
        assert _snap() == endv, (b,)

    es2a.close()
    es2ps.close()

    wo_sb = [sbt(f"wo_sb{i}", [128, D], BF16, es3) for i in range(2)]
    wf3 = [sbt(f"wf3_{i}", [128, 2048], F32, es3) for i in range(2)]
    qa_sb = sbt("qa_sb", [128, 2048], BF16, es3)
    qb_sb = sbt("qb_sb", [128, 2048], BF16, es3)
    wq3 = [sbt(f"wq3_{i}", [128, D], BF16, es3) for i in range(2)]
    xT_sb = [sbt(f"xT_sb{i}", [128, 512], F32, es3) for i in range(2)]
    ffr_sb = sbt("ffr_sb", [128, 512], F32, es3)
    out_sb = [sbt(f"out_sb{i}", [128, 512], F32, es3) for i in range(2)]

    fps = [pst(f"fps{i}", [128, 512], F32, es3) for i in range(2)]
    ops_ = [pst(f"ops{i}", [128, 512], F32, es3) for i in range(2)]

    def b3_pass(em, L):
        def sync_stream(s):
            for tc in range(2):
                ins = None
                if em:
                    ins = s.dma_start(
                        out=xT_sb[tc][:],
                        in_=xTd[0:128, tc * 512:(tc + 1) * 512])
                L[("xtd", 0, tc)] = s_xt3[tc].inc(ins, 16)
            for cc in range(32):
                nx = cc + 1
                if nx < 32:
                    for tc in range(2):
                        if em:
                            s.wait_ge(s_v.h, L[("ffr", cc, tc)])
                        ins = None
                        if em:
                            ins = s.dma_start(
                                out=xT_sb[tc][:],
                                in_=xTd[nx * 128:(nx + 1) * 128,
                                        tc * 512:(tc + 1) * 512])
                        L[("xtd", nx, tc)] = s_xt3[tc].inc(ins, 16)
                for tc in range(2):
                    if em:
                        s.wait_ge(s_v.h, L[("fin", cc, tc)])
                    ins = None
                    if em:
                        ins = s.dma_start(
                            out=outT[cc * 128:(cc + 1) * 128,
                                     tc * 512:(tc + 1) * 512],
                            in_=out_sb[tc][:])
                    L[("st", cc, tc)] = s_st2[tc].inc(ins, 16)
            if em:
                s.wait_ge(s_st2[0].h, L[("st", 31, 0)])
                s.wait_ge(s_st2[1].h, L[("st", 31, 1)])

        def gpsimd_stream(gp):
            for cc in range(2):
                ins = gp.dma_start(out=wo_sb[cc % 2][:], in_=woT[cc]) if em else None
                L[("wod", cc)] = s_wo[cc % 2].inc(ins, 16)
            for cc in range(32):
                nx = cc + 2
                if nx < 32:
                    if em:
                        gp.wait_ge(s_t.h, L[("to", cc, 1)])
                    ins = gp.dma_start(out=wo_sb[nx % 2][:], in_=woT[nx]) if em else None
                    L[("wod", nx)] = s_wo[nx % 2].inc(ins, 16)

        def scalar_stream3(sc):
            for half in range(2):
                ins = None
                if em:
                    ins = sc.dma_start(
                        out=wf3[half][:],
                        in_=wfTd[0][:, half * 2048:(half + 1) * 2048])
                L[("wfd", 0, half)] = s_wf3[half].inc(ins, 16)
            for cc in range(32):
                nx = cc + 1
                if nx < 32:
                    for half in range(2):
                        if em:
                            sc.wait_ge(s_v.h, L[("qsub", cc, half)])
                        ins = None
                        if em:
                            ins = sc.dma_start(
                                out=wf3[half][:],
                                in_=wfTd[nx][:, half * 2048:(half + 1) * 2048])
                        L[("wfd", nx, half)] = s_wf3[half].inc(ins, 16)

        def vector_stream(v):
            def quant(q):
                for half in range(2):
                    if em:
                        v.wait_ge(s_wf3[half].h, L[("wfd", q, half)])
                        if q >= 2:
                            v.wait_ge(s_t.h, L[("tf", q - 2, 1)])
                        if (q, half) != (0, 0):
                            pq = (q, 0) if half == 1 else (q - 1, 1)
                            v.wait_ge(s_v.h, L[("qsub", pq[0], pq[1])])
                    la = s_v.inc(v.tensor_scalar(
                        qa_sb[:], wf3[half][:], consts[:, 2:3], None,
                        op0=ALU.is_gt) if em else None)
                    lb = s_v.inc(v.tensor_scalar(
                        qb_sb[:], wf3[half][:], consts[:, 3:4], None,
                        op0=ALU.is_lt) if em else None)
                    if em:
                        v.wait_ge(s_v.h, lb)
                    L[("qsub", q, half)] = s_v.inc(v.tensor_sub(
                        wq3[q % 2][:, half * 2048:(half + 1) * 2048],
                        qa_sb[:], qb_sb[:]) if em else None)
            quant(0)
            for cc in range(32):
                if cc + 1 < 32:
                    quant(cc + 1)
                for tc in range(2):
                    if em:
                        v.wait_ge(s_t.h, L[("tf", cc, tc)])
                        v.wait_ge(s_xt3[tc].h, L[("xtd", cc, tc)])
                        if (cc, tc) != (0, 0):
                            pf = (cc, 0) if tc == 1 else (cc - 1, 1)
                            v.wait_ge(s_v.h, L[("fin", pf[0], pf[1])])
                    L[("ffr", cc, tc)] = s_v.inc(v.scalar_tensor_tensor(
                        ffr_sb[:], fps[tc][:], consts[:, 1:2], xT_sb[tc][:],
                        op0=ALU.mult, op1=ALU.add) if em else None)
                    if em:
                        v.wait_ge(s_t.h, L[("to", cc, tc)])
                        v.wait_ge(s_v.h, L[("ffr", cc, tc)])
                        if cc > 0:
                            v.wait_ge(s_st2[tc].h, L[("st", cc - 1, tc)])
                    L[("fin", cc, tc)] = s_v.inc(v.scalar_tensor_tensor(
                        out_sb[tc][:], ops_[tc][:], obias[:, cc:cc + 1], ffr_sb[:],
                        op0=ALU.add, op1=ALU.add) if em else None)

        def tensor_stream(t):
            for cc in range(32):
                if em:
                    t.wait_ge(s_wo[cc % 2].h, L[("wod", cc)])
                for tc in range(2):
                    if em:
                        t.wait_ge(s_v.h, L[("qsub", cc, 1)])
                        if cc > 0:
                            t.wait_ge(s_v.h, L[("ffr", cc - 1, tc)])
                    last = None
                    for dc in range(32):
                        if em:
                            last = t.matmul(
                                fps[tc][:], wq3[cc % 2][:, dc * 128:(dc + 1) * 128],
                                xnTo.ap()[:, dc * 1024 + tc * 512:
                                          dc * 1024 + (tc + 1) * 512],
                                start=(dc == 0), stop=(dc == 31))
                    L[("tf", cc, tc)] = s_t.inc(last if em else None)
                    if em and cc > 0:
                        t.wait_ge(s_v.h, L[("fin", cc - 1, tc)])
                    last = None
                    for h in range(32):
                        if em:
                            last = t.matmul(
                                ops_[tc][:], wo_sb[cc % 2][:, h * 128:(h + 1) * 128],
                                attnT.ap()[:, h * 1024 + tc * 512:
                                           h * 1024 + (tc + 1) * 512],
                                start=(h == 0), stop=(h == 31))
                    L[("to", cc, tc)] = s_t.inc(last if em else None)

        return (sync_stream, gpsimd_stream, vector_stream, tensor_stream,
                scalar_stream3)

    L3 = {}

    def _snap3():
        return (s_v.v, s_s.v, s_t.v, s_st2[0].v, s_st2[1].v,
                s_wo[0].v, s_wo[1].v, s_wf3[0].v, s_wf3[1].v,
                s_xt3[0].v, s_xt3[1].v)

    def _restore3(t_):
        (s_v.v, s_s.v, s_t.v, s_st2[0].v, s_st2[1].v,
         s_wo[0].v, s_wo[1].v, s_wf3[0].v, s_wf3[1].v,
         s_xt3[0].v, s_xt3[1].v) = t_

    save = _snap3()
    sy, gp_, vv, tt_, sc_ = b3_pass(False, L3)
    gp_(None); sc_(None); vv(None); tt_(None); sy(None)
    endv = _snap3()
    _restore3(save)
    sy, gp_, vv, tt_, sc_ = b3_pass(True, L3)
    with nc.Block() as blk:
        blk.sync(sy)
        blk.gpsimd(gp_)
        blk.scalar(sc_)
        blk.vector(vv)
        blk.tensor(tt_)
    assert _snap3() == endv

    return nc


def _storage_order(parity):
    if parity == 0:
        return [15 - i for i in range(16)]
    o = []
    for j in range(8):
        o += [14 - 2 * j, 15 - 2 * j]
    return o


_NC_CACHE = None


def kernel(L0_attn_x, L0_attn_wq_w, L0_attn_wq_b, L0_attn_wk_w, L0_attn_wk_b,
           L0_attn_wv_w, L0_attn_wv_b, L0_attn_wo_w, L0_attn_wo_b,
           L0_ffn_weight, L0_ffn_bias):
    import ml_dtypes
    global _NC_CACHE

    x = np.ascontiguousarray(np.asarray(L0_attn_x, dtype=np.float32))
    wq = np.asarray(L0_attn_wq_w, np.float32)
    wk = np.asarray(L0_attn_wk_w, np.float32)
    wv = np.asarray(L0_attn_wv_w, np.float32)
    wo = np.asarray(L0_attn_wo_w, np.float32)
    wf = np.asarray(L0_ffn_weight, np.float32)

    wqT = np.ascontiguousarray(wq.reshape(32, 128, 32, 128).transpose(0, 3, 2, 1)).reshape(32, 128, D)
    wkT = np.ascontiguousarray(
        wk.reshape(128, 32, 128).transpose(2, 1, 0)).reshape(128, D)
    wvT = np.ascontiguousarray(
        wv.reshape(128, 32, 128).transpose(2, 1, 0)).reshape(128, D)
    woT = np.ascontiguousarray(wo.reshape(32, 128, 32, 128).transpose(0, 3, 2, 1)).reshape(32, 128, D)
    wfT = np.ascontiguousarray(wf.reshape(32, 128, 32, 128).transpose(0, 3, 2, 1)).reshape(32, 128, D)
    wqb = np.ascontiguousarray(np.asarray(L0_attn_wq_b, np.float32).reshape(32, 128).T)
    wkb = np.ascontiguousarray(np.asarray(L0_attn_wk_b, np.float32).reshape(128, 1))
    wvb = np.ascontiguousarray(np.asarray(L0_attn_wv_b, np.float32).reshape(128, 1))
    wob = np.ascontiguousarray(np.asarray(L0_attn_wo_b, np.float32).reshape(32, 128).T)
    fbb = np.ascontiguousarray(np.asarray(L0_ffn_bias, np.float32).reshape(32, 128).T)
    identb = np.eye(128, dtype=np.float32).astype(ml_dtypes.bfloat16)
    identf = np.eye(128, dtype=np.float32)

    r = np.arange(128)
    tri = np.where(r[:, None] >= r[None, :], 0.0, -1.0e30).astype(np.float32)
    zeros = np.zeros((128, 128), np.float32)
    minf = np.full((128, 128), -1.0e30, np.float32)

    in_maps = []
    perms = []
    for c in range(8):
        b, parity = c // 2, c % 2
        sigma = _storage_order(parity)
        perm = np.concatenate([np.arange(g * 128, (g + 1) * 128) for g in sigma])
        perms.append(perm)
        xb = x[b]
        xtok_c = np.ascontiguousarray(xb[perm])
        own_rows = np.concatenate(
            [perm[2 * s * 128:(2 * s + 1) * 128] for s in range(8)])
        xT_c = np.ascontiguousarray(xb[own_rows].T)
        mask2_c = np.ascontiguousarray(
            np.concatenate([tri, zeros if parity == 0 else minf], axis=1))
        wfsl_c = np.ascontiguousarray(wf.reshape(8, -1)[c].reshape(128, 16384))
        in_maps.append({
            "xtok": xtok_c, "xT": xT_c, "wqT": wqT, "wkT": wkT, "wvT": wvT,
            "woT": woT, "wfT": wfT, "wfsl": wfsl_c, "wqb": wqb, "wkb": wkb,
            "wvb": wvb, "wob": wob, "fbb": fbb, "mask2": mask2_c,
            "identb": identb, "identf": identf,
        })

    if _NC_CACHE is None:
        _NC_CACHE = build_nc()
    nc = _NC_CACHE

    res = run_bass_kernel_spmd(nc, in_maps, core_ids=list(range(8)))

    out = np.empty((B, T, D), dtype=np.float32)
    for c in range(8):
        b = c // 2
        perm = perms[c]
        own_rows = np.concatenate(
            [perm[2 * s * 128:(2 * s + 1) * 128] for s in range(8)])
        out[b][own_rows] = res.results[c]["outT"].T
    return out
